# version 11
# baseline (speedup 1.0000x reference)
import numpy as np
import concourse.bass as bass
import concourse.mybir as mybir
from concourse.bass_utils import run_bass_kernel_spmd
from contextlib import ExitStack

F32 = mybir.dt.float32
BF16 = mybir.dt.bfloat16
AF = mybir.ActivationFunctionType
ALU = mybir.AluOpType
AX = mybir.AxisListType

NCORES = 8
NSEQ = 2
T = 2048
D = 2048
H = 16
HD = 128
G = 4
TG = 512
FC = 16
FFH = 5632
OC = FFH // 128
LNEPS = 1e-5
QEPS = 1e-8
ENGS = ("sy", "g", "t", "s", "v")
DBG = []


class Sem:
    __slots__ = ("h", "n")

    def __init__(self, nc, name):
        self.h = nc.alloc_semaphore(name=name)
        self.n = 0


class Builder:
    def __init__(self, nc):
        self.nc = nc
        self.prog = {k: [] for k in ENGS}
        self._sems = {}

    def sem(self, name):
        if name not in self._sems:
            self._sems[name] = Sem(self.nc, name)
        return self._sems[name]

    def w(self, eng, s, val=None):
        v = s.n if val is None else val
        if v <= 0:
            return
        self.prog[eng].append(lambda e, h=s.h, v=v: e.wait_ge(h, v))

    def op(self, eng, fn, inc=None, k=1, inc2=None, k2=1):
        if inc is not None:
            inc.n += k
        if inc2 is not None:
            inc2.n += k2
        def run(e, fn=fn, inc=inc, k=k, inc2=inc2, k2=k2):
            ins = fn(e)
            if inc is not None:
                ins = ins.then_inc(inc.h, k)
            if inc2 is not None:
                ins.then_inc(inc2.h, k2)
        self.prog[eng].append(run)

    def dma(self, eng, out_ap, in_ap, inc):
        self.op(eng, lambda e, o=out_ap, i=in_ap: e.dma_start(o, i), inc, 16)

    def barrier(self, marks, engs=ENGS):
        for eng in engs:
            for s in marks:
                self.w(eng, s, s.n)


def _sb(ctx, nc, name, shape, dtype):
    return ctx.enter_context(nc.sbuf_tensor(name, shape, dtype))


class RowLN:

    def __init__(self, B, ctx, tag):
        nc = B.nc
        self.B = B
        self.junk = _sb(ctx, nc, f"{tag}_junk", [128, 2048], F32)
        self.sums = _sb(ctx, nc, f"{tag}_sums", [128, 1], F32)
        self.mean = _sb(ctx, nc, f"{tag}_mean", [128, 1], F32)
        self.msq = _sb(ctx, nc, f"{tag}_msq", [128, 1], F32)
        self.ssq = [_sb(ctx, nc, f"{tag}_ssq{i}", [128, 1], F32) for i in (0, 1)]
        self.var = [_sb(ctx, nc, f"{tag}_var{i}", [128, 1], F32) for i in (0, 1)]
        self.std = [_sb(ctx, nc, f"{tag}_std{i}", [128, 1], F32) for i in (0, 1)]
        self.rstd = [_sb(ctx, nc, f"{tag}_rstd{i}", [128, 1], F32) for i in (0, 1)]
        self.nmr = [_sb(ctx, nc, f"{tag}_nmr{i}", [128, 1], F32) for i in (0, 1)]
        self.ss1 = B.sem(f"{tag}_ss1")
        self.sv1 = B.sem(f"{tag}_sv1")
        self.ss2 = B.sem(f"{tag}_ss2")
        self.sv2 = B.sem(f"{tag}_sv2")
        self.ssa = B.sem(f"{tag}_ssa")
        self.svx = B.sem(f"{tag}_svx")
        self.uses = 0

    def emit(self, src_ap, apply_src_ap, out_ap, fd, epsap,
             act_prewaits=(), dve_prewaits=()):
        B = self.B
        p = self.uses % 2
        self.uses += 1
        if self.uses > 2:
            B.w("s", self.sv1)
            B.w("s", self.sv2)
            B.w("v", self.ss2)
            B.w("v", self.ssa)
        for s_, v_ in act_prewaits:
            B.w("s", s_, v_)
        for s_, v_ in dve_prewaits:
            B.w("v", s_, v_)
        B.op("s", lambda e, s=src_ap, j=self.junk, q=self.ssq[p], f=fd:
             e.activation(j[:, 0:f], s, AF.Square, accum_out=q[:, :]),
             inc=self.ss1)
        B.op("v", lambda e, s=src_ap, o=self.sums:
             e.reduce_sum(o[:, :], s, AX.X), inc=self.svx)
        B.w("v", self.svx)
        B.op("v", lambda e, i=self.sums, o=self.mean, c=1.0 / fd:
             e.tensor_single_scalar(o[:, :], i[:, :], c, ALU.mult),
             inc=self.svx)
        B.w("v", self.svx)
        B.op("v", lambda e, m=self.mean, o=self.msq:
             e.scalar_tensor_tensor(o[:, :], m[:, :], 1.0, m[:, :],
                                    ALU.mult, ALU.mult), inc=self.svx)
        B.w("v", self.ss1)
        B.w("v", self.svx)
        B.op("v", lambda e, q=self.ssq[p], m=self.msq, o=self.var[p], c=1.0 / fd:
             e.scalar_tensor_tensor(o[:, :], q[:, :], c, m[:, :],
                                    ALU.mult, ALU.subtract), inc=self.sv1)
        B.w("s", self.sv1)
        B.op("s", lambda e, i=self.var[p], o=self.std[p], b=epsap:
             e.activation(o[:, :], i[:, :], AF.Sqrt, bias=b[:, :]),
             inc=self.ss2)
        B.w("v", self.ss2)
        B.op("v", lambda e, i=self.std[p], o=self.rstd[p]:
             e.reciprocal(o[:, :], i[:, :]), inc=self.svx)
        B.w("v", self.svx)
        B.op("v", lambda e, m=self.mean, r=self.rstd[p], o=self.nmr[p]:
             e.scalar_tensor_tensor(o[:, :], m[:, :], -1.0, r[:, :],
                                    ALU.mult, ALU.mult), inc=self.sv2)
        B.w("s", self.sv2)
        B.op("s", lambda e, i=apply_src_ap, o=out_ap, r=self.rstd[p],
             n=self.nmr[p]:
             e.activation(o, i, AF.Identity, bias=n[:, :], scale=r[:, :]),
             inc=self.ssa)


def build_program():
    nc = bass.Bass("TRN2", target_bir_lowering=False)
    B = Builder(nc)

    x_in = nc.dram_tensor("x", [NSEQ * T, D], F32, kind="ExternalInput").ap()
    wqkv = nc.dram_tensor("wqkv", [D, 3 * D], F32, kind="ExternalInput").ap()
    wout = nc.dram_tensor("wout", [D, D], F32, kind="ExternalInput").ap()
    lnsc_in = nc.dram_tensor("lnsc", [128, 1], F32, kind="ExternalInput").ap()
    lnbi_in = nc.dram_tensor("lnbi", [128, 1], F32, kind="ExternalInput").ap()
    w1tt = nc.dram_tensor("w1tt", [OC, FC, 128, 128], F32, kind="ExternalInput").ap()
    w2tt = nc.dram_tensor("w2tt", [OC, FC, 128, 128], F32, kind="ExternalInput").ap()
    w3t = nc.dram_tensor("w3t", [FFH, D], F32, kind="ExternalInput").ap()
    cosf_in = nc.dram_tensor("cosf", [128, T], F32, kind="ExternalInput").ap()
    sinf_in = nc.dram_tensor("sinf", [128, T], F32, kind="ExternalInput").ap()
    ident_in = nc.dram_tensor("identm", [128, 128], F32, kind="ExternalInput").ap()
    masks_in = nc.dram_tensor("masks", [128, 4 * TG], F32, kind="ExternalInput").ap()

    y_out = nc.dram_tensor("y", [NSEQ * T, D], F32, kind="ExternalOutput").ap()

    def scr(name, shape, dtype):
        kind = "ExternalOutput" if name in DBG else "Internal"
        return nc.dram_tensor(name, shape, dtype, kind=kind).ap()

    s_ln1f = scr("s_ln1f", [NSEQ, G, FC, 128, TG], BF16)
    s_q = scr("s_q", [NSEQ, H, G, 128, TG], BF16)
    s_k = scr("s_k", [NSEQ, H, G, 128, TG], BF16)
    s_v = scr("s_v", [NSEQ, H, G, 128, TG], BF16)
    s_xnew = scr("s_xnew", [NSEQ * T, D], F32)
    s_hid = scr("s_hid", [NSEQ * T // 512, 128, OC, 512], BF16)
    s_w1q = scr("s_w1q", [OC, FC, 128, 128], BF16)
    s_w2q = scr("s_w2q", [OC, FC, 128, 128], BF16)
    s_w3q = scr("s_w3q", [4, 128, OC, TG], BF16)

    top = ExitStack()
    with top:
        ident = _sb(top, nc, "ident", [128, 128], BF16)
        ones128 = _sb(top, nc, "ones128", [128, 128], BF16)
        onescol = _sb(top, nc, "onescol", [128, 1], BF16)
        onesrow = _sb(top, nc, "onesrow", [1, 128], BF16)
        onesf = _sb(top, nc, "onesf", [128, 128], F32)
        lnsc = _sb(top, nc, "lnsc_t", [128, 1], F32)
        lnbi = _sb(top, nc, "lnbi_t", [128, 1], F32)
        epsv = _sb(top, nc, "epsv", [128, 1], F32)
        tau = [_sb(top, nc, f"tau{i}", [128, 1], F32) for i in range(3)]
        ntau = [_sb(top, nc, f"ntau{i}", [128, 1], F32) for i in range(3)]
        gpart = _sb(top, nc, "gpart", [128, 136], F32)

        pbf = [top.enter_context(nc.psum_tensor(f"pbf{i}", [128, 512], F32))
               for i in range(6)]
        pbt = [top.enter_context(nc.psum_tensor(f"pbt{i}", [128, 1024], BF16))
               for i in range(2)]

        rln = RowLN(B, top, "rln")

        dC = B.sem("dC")
        dG = B.sem("dG")
        B.op("g", lambda e: e.memset(ones128[:, :], 1.0), inc=dG)
        B.op("g", lambda e: e.memset(onescol[:, :], 1.0), inc=dG)
        B.op("g", lambda e: e.memset(onesrow[:, :], 1.0), inc=dG)
        B.op("g", lambda e: e.memset(onesf[:, :], 1.0), inc=dG)
        B.op("g", lambda e: e.memset(epsv[:, :], LNEPS), inc=dG)
        dI = B.sem("dI")
        B.dma("g", ident[:, :], ident_in, dI)
        B.dma("sy", lnsc[:, :], lnsc_in, dC)
        B.dma("sy", lnbi[:, :], lnbi_in, dC)
        B.w("t", dG, 6)
        B.w("t", dI, 16)
        B.w("s", dG, 6)
        B.w("v", dG, 6)
        B.w("s", dC, 32)
        cneg = _sb(top, nc, "cneg", [128, 1], F32)
        nc.const_aps.register(F32, -0.5, cneg[:, :])
        B.op("g", lambda e: e.memset(cneg[:, :], -0.5), inc=dG)

        with ExitStack() as p0:
            CW = 4096
            wcb = [_sb(p0, nc, f"p0_w{i}", [128, CW], F32) for i in (0, 1)]
            lsb = [_sb(p0, nc, f"p0_l{i}", [128, CW], F32) for i in (0, 1)]
            gjk = [_sb(p0, nc, f"p0_gjk{i}", [128, CW], F32) for i in (0, 1)]
            qb = [_sb(p0, nc, f"p0_q{i}", [128, CW], BF16) for i in (0, 1)]

            ld = [B.sem("p0_ld0"), B.sem("p0_ld1")]
            fr = [B.sem("p0_fr0"), B.sem("p0_fr1")]
            qwr = [B.sem("p0_qw0"), B.sem("p0_qw1")]
            svch = B.sem("p0_svch")
            sgch = B.sem("p0_sgch")
            sgam = B.sem("p0_sgam")
            svg = B.sem("p0_svg")
            stg = B.sem("p0_stg")

            def w12_src(arr):
                return lambda c, a=arr: (
                    a[2 * c:2 * c + 2].rearrange("oc fc p o -> p oc fc o"))

            def w12_dst(arr):
                return lambda c, a=arr: (
                    a[2 * c:2 * c + 2].rearrange("oc fc p o -> p oc fc o"))

            def w3_src(c):
                return w3t.rearrange("(hc p) j -> p hc j", p=128)[
                    :, 2 * c:2 * c + 2, :]

            def w3_dst(c):
                return s_w3q[:, :, 2 * c:2 * c + 2, :].rearrange(
                    "jg p hc j -> p hc jg j")

            mats = [
                ("w1", OC // 2, w12_src(w1tt), w12_dst(s_w1q), 3),
                ("w2", OC // 2, w12_src(w2tt), w12_dst(s_w2q), 3),
                ("w3", OC // 2, w3_src, w3_dst, 2),
            ]

            def buf_view(b, nd, split_j=False):
                if nd == 3:
                    return b[:, :].rearrange("p (oc fc o) -> p oc fc o",
                                             oc=2, fc=FC)
                if split_j:
                    return b[:, :].rearrange("p (hc jg j) -> p hc jg j",
                                             hc=2, jg=4)
                return b[:, :].rearrange("p (hc j) -> p hc j", hc=2)

            col = 0
            gcols = {}
            for mi, (name, nch, srcf, dstf, nd) in enumerate(mats):
                gcols[name] = (col, nch)
                for c in range(nch):
                    p = c % 2
                    lq = "sy" if p == 0 else "s"
                    if ld[p].n > 0:
                        B.w(lq, fr[p])
                    B.dma(lq, buf_view(wcb[p], nd), srcf(c), ld[p])
                    if p == 0:
                        B.w("s", ld[p])
                        B.w("s", fr[p])
                        B.op("s", lambda e, b=wcb[p], o=gpart, cc=col:
                             e.activation(gjk[0][:, :], b[:, :], AF.Abs,
                                          accum_out=o[:, cc:cc + 1]),
                             inc=fr[p])
                    else:
                        B.w("v", ld[p])
                        B.op("v", lambda e, b=wcb[p], o=gpart, cc=col:
                             e.reduce_sum(o[:, cc:cc + 1], b[:, :], AX.X,
                                          apply_absolute_value=True),
                             inc=fr[p])
                    col += 1
            B.op("s", lambda e: e.activation(gjk[1][:, 0:1], gpart[:, 0:1],
                                             AF.Copy), inc=sgam)
            B.w("v", sgam)
            B.w("v", fr[1], fr[1].n)
            for mi, (name, nch, _, _, _) in enumerate(mats):
                c0, n_ = gcols[name]
                B.op("v", lambda e, a=c0, b2=c0 + n_, o=132 + mi:
                     e.reduce_sum(gpart[:, o:o + 1], gpart[:, a:b2], AX.X),
                     inc=svg)
            B.w("t", svg, 3)
            for mi in range(3):
                B.op("t", lambda e, o=132 + mi, b=pbf[mi]:
                     e.matmul(b[:, 0:1], onesf[:, :], gpart[:, o:o + 1],
                              start=True, stop=True), inc=stg)
            B.w("v", stg, 3)
            for mi, (name, nch, _, _, _) in enumerate(mats):
                n_el = OC * 128 * 2048
                B.op("v", lambda e, b=pbf[mi], t_=tau[mi], c=0.5 / n_el:
                     e.tensor_scalar(t_[:, :], b[:, 0:1], c, 0.5 * QEPS,
                                     ALU.mult, ALU.add), inc=svch)
                B.w("v", svch)
                B.op("v", lambda e, t_=tau[mi], u=ntau[mi]:
                     e.tensor_single_scalar(u[:, :], t_[:, :], -1.0, ALU.mult),
                     inc=svg)
            B.w("v", svg, 6)
            B.w("g", svg, 6)

            for mi, (name, nch, srcf, dstf, nd) in enumerate(mats):
                for c in range(nch):
                    p = c % 2
                    eng = "v"
                    chain = svch
                    lq = "sy" if p == 0 else "s"
                    B.w(lq, fr[p])
                    B.dma(lq, buf_view(wcb[p], nd), srcf(c), ld[p])
                    B.w(eng, ld[p])
                    if qwr[p].n > 0:
                        B.w(eng, qwr[p])
                    B.op(eng, lambda e, b=wcb[p], u=ntau[mi], l=lsb[p]:
                         e.tensor_single_scalar(l[:, :], b[:, :],
                                                u[:, :], ALU.is_lt),
                         inc=chain)
                    B.w(eng, chain)
                    B.op(eng, lambda e, b=wcb[p], t_=tau[mi], q=qb[p], l=lsb[p]:
                         e.scalar_tensor_tensor(q[:, :], b[:, :], t_[:, :],
                                                l[:, :], ALU.is_gt,
                                                ALU.subtract),
                         inc=fr[p])
                    B.w("g", fr[p])
                    if nd == 2:
                        for sub in range(2):
                            B.dma("g", dstf(c)[:, sub],
                                  qb[p][:, sub * 2048:(sub + 1) * 2048]
                                  .rearrange("p (jg j) -> p jg j", jg=4),
                                  qwr[p])
                    else:
                        B.dma("g", dstf(c), buf_view(qb[p], nd), qwr[p])
            B.barrier([fr[0], fr[1], qwr[0], qwr[1], svg])

        with ExitStack() as p1:
            cosf = _sb(p1, nc, "cosf_sb", [128, T], F32)
            sinf = _sb(p1, nc, "sinf_sb", [128, T], F32)
            wqb_t = _sb(p1, nc, "wqb_t", [128, FC * 2048], BF16)
            ln1f_t = _sb(p1, nc, "ln1f_t", [128, FC * TG], BF16)
            ln1f = [ln1f_t[:, i * TG:(i + 1) * TG] for i in range(FC)]
            xt = [_sb(p1, nc, f"xt{i}", [128, D], F32) for i in (0, 1)]
            ln1t = [_sb(p1, nc, f"ln1t{i}", [128, D], BF16) for i in (0, 1)]
            q_s = [_sb(p1, nc, f"qs{i}", [128, TG], F32) for i in (0, 1, 2)]
            t2 = [_sb(p1, nc, f"t2_{i}", [128, TG], F32) for i in (0, 1, 2)]
            c1 = _sb(p1, nc, "c1", [128, TG], F32)
            t2m = _sb(p1, nc, "t2m", [128, TG], F32)
            qrb = [_sb(p1, nc, f"qrb{i}", [128, TG], BF16) for i in (0, 1, 2)]
            q2b = [_sb(p1, nc, f"q2b{i}", [128, TG], BF16) for i in (0, 1, 2)]
            mtl = _sb(p1, nc, "mtl", [128, TG], F32)
            e2t = _sb(p1, nc, "e2t", [128, TG], F32)
            msqt = _sb(p1, nc, "msqt", [128, TG], F32)
            vart = [_sb(p1, nc, f"vart{i}", [128, TG], F32) for i in (0, 1, 2)]
            stdt = [_sb(p1, nc, f"stdt{i}", [128, TG], F32) for i in (0, 1, 2)]
            rstdt = [_sb(p1, nc, f"rstdt{i}", [128, TG], F32) for i in (0, 1, 2)]
            t1b = _sb(p1, nc, "t1b", [128, TG], BF16)
            t2b = [_sb(p1, nc, f"t2b{i}", [128, TG], BF16) for i in (0, 1, 2)]
            qp = [_sb(p1, nc, f"qp{i}", [128, TG], BF16) for i in (0, 1, 2)]

            dcs = B.sem("p1_dcs")
            B.dma("sy", cosf[:, :], cosf_in, dcs)
            B.dma("sy", sinf[:, :], sinf_in, dcs)
            B.w("v", dcs, 32)

            wql = B.sem("p1_wql")
            xl = [B.sem("p1_xl0"), B.sem("p1_xl1")]
            trw = [B.sem("p1_trw0"), B.sem("p1_trw1")]
            trf = [B.sem("p1_trf0"), B.sem("p1_trf1")]
            stpr = [B.sem(f"p1_stpr{i}") for i in range(3)]
            ssev = [B.sem(f"p1_ssev{i}") for i in range(3)]
            svqr = [B.sem(f"p1_svqr{i}") for i in range(3)]
            ssq2 = [B.sem(f"p1_ssq2{i}") for i in range(3)]
            sts = B.sem("p1_sts0")
            stq2 = B.sem("p1_stq20")
            svs = B.sem("p1_svs0b")
            svq2f = B.sem("p1_svq2f0")
            svvar = [B.sem(f"p1_svvar{i}") for i in range(3)]
            ssstd = [B.sem(f"p1_ssstd{i}") for i in range(3)]
            svt2b = [B.sem(f"p1_svt2b{i}") for i in range(3)]
            ssqp = [B.sem(f"p1_ssqp{i}") for i in range(3)]
            dqwr = [B.sem(f"p1_dqwr{i}") for i in range(3)]
            last_sts = [0, 0, 0]
            last_stq2 = [0, 0, 0]
            dln1w = B.sem("p1_dln1w")
            dln1r = B.sem("p1_dln1r")
            stprog = B.sem("p1_stprog")

            bankfree = [None, None, None]

            def emit_ln1_group(s, g, produce):
                if not produce:
                    B.w("sy", stprog)
                    B.w("sy", dln1w)
                    B.dma("sy",
                          ln1f_t[:, :].rearrange("p (fc w) -> p fc w", fc=FC),
                          s_ln1f[s, g].rearrange("fc p w -> p fc w"), dln1r)
                    return
                first_grp = (s == 0 and g == 0)
                for tt in range(4):
                    p = tt % 2
                    row = s * T + g * TG + tt * 128
                    if xl[p].n > 0:
                        B.w("sy", rln.ssa, rln.ssa.n)
                    B.dma("sy", xt[p][:, :], x_in[row:row + 128, :], xl[p])
                    if not (first_grp and tt < 2):
                        B.w("s", trw[0])
                        B.w("s", trw[1])
                    rln.emit(xt[p][:, :], xt[p][:, :], ln1t[p][:, :], D, epsv,
                             act_prewaits=[(xl[p], xl[p].n)],
                             dve_prewaits=[(xl[p], xl[p].n)])
                    B.w("t", rln.ssa)
                    if tt == 0:
                        B.w("v", stprog)
                        B.w("v", dln1w)
                    for fc in range(FC):
                        q = fc % 2
                        if trw[q].n > 0:
                            B.w("t", trf[q])
                        B.op("t", lambda e, i=ln1t[p], f=fc, b=pbt[q]:
                             e.transpose(b[:, 0:128],
                                         i[:, f * 128:(f + 1) * 128],
                                         ident[:, :]), inc=trw[q])
                        B.w("v", trw[q])
                        B.op("v", lambda e, b=pbt[q], f=fc, o=tt * 128:
                             e.tensor_copy(ln1f_t[:, f * TG + o:f * TG + o + 128],
                                           b[:, 0:128]),
                             inc=trf[q])
                B.w("g", trf[0])
                B.w("g", trf[1])
                B.dma("g", s_ln1f[s, g].rearrange("fc p w -> p fc w"),
                      ln1f_t[:, :].rearrange("p (fc w) -> p fc w", fc=FC),
                      dln1w)

            def emit_pass(kind):
                off = {"q": 0, "k": D, "v": 2 * D}[kind]
                B.w("g", stprog)
                for fq in range(4):
                    B.dma("g",
                          wqb_t[:, fq * 4 * 2048:(fq + 1) * 4 * 2048]
                          .rearrange("p (fc j) -> p fc j", fc=4),
                          wqkv[:, off:off + D]
                          .rearrange("(fc p) j -> p fc j", p=128)
                          [:, fq * 4:(fq + 1) * 4, :], wql)
                hidx = 0
                pend_stats = []

                def flush_stats():
                    while pend_stats:
                        B.prog["t"].append(pend_stats.pop(0))
                for s in range(NSEQ):
                    for g in range(G):
                        emit_ln1_group(s, g, produce=(kind == "q"))
                        B.w("t", wql)
                        if kind == "q":
                            B.w("t", trf[0])
                            B.w("t", trf[1])
                        else:
                            B.w("t", dln1r)
                        sdst = {"q": s_q, "k": s_k, "v": s_v}[kind]
                        for h in range(H):
                            p = hidx % 3
                            hidx += 1
                            if bankfree[p] is not None:
                                B.w("t", *bankfree[p])
                            for fc in range(FC):
                                B.op("t", lambda e, f=fc, hh=h, b=pbf[p],
                                     st=(fc == 0), sp=(fc == FC - 1):
                                     e.matmul(b[:, :],
                                              wqb_t[:, f * 2048 + hh * 128:f * 2048 + (hh + 1) * 128],
                                              ln1f_t[:, f * TG:(f + 1) * TG], start=st, stop=sp),
                                     inc=(stpr[p] if fc == FC - 1 else None))
                            flush_stats()
                            if kind == "v":
                                B.w("s", stpr[p])
                                if dqwr[p].n > 0:
                                    B.w("s", dqwr[p])
                                B.op("s", lambda e, b=pbf[p], o=qp[p]:
                                     e.activation(o[:, :], b[:, :], AF.Copy),
                                     inc=ssqp[p])
                                bankfree[p] = (ssqp[p], ssqp[p].n)
                                B.w("g", ssqp[p])
                                B.dma("g", sdst[s, h, g], qp[p][:, :], dqwr[p])
                                continue
                            B.w("s", stpr[p])
                            if svqr[p].n > 0:
                                B.w("s", svqr[p])
                            B.op("s", lambda e, b=pbf[p], o=q_s[p]:
                                 e.activation(o[:, :], b[:, :], AF.Copy))
                            B.op("s", lambda e, i=q_s[p], o=t2[p]:
                                 e.activation(o[0:64, :], i[64:128, :],
                                              AF.Copy))
                            B.op("s", lambda e, i=q_s[p], o=t2[p]:
                                 e.activation(o[64:128, :], i[0:64, :],
                                              AF.Copy), inc=ssev[p])
                            bankfree[p] = (ssev[p], ssev[p].n)
                            gs = g * TG
                            B.w("v", ssev[p])
                            if ssq2[p].n > 0:
                                B.w("v", ssq2[p])
                                B.w("v", sts, last_sts[p])
                            B.op("v", lambda e, i=q_s[p], a=gs:
                                 e.tensor_mul(c1[:, :], i[:, :],
                                              cosf[:, a:a + TG]))
                            B.op("v", lambda e, i=t2[p], a=gs:
                                 e.tensor_mul(t2m[:, :], i[:, :],
                                              sinf[:, a:a + TG]))
                            B.op("v", lambda e, o=qrb[p]:
                                 e.tensor_add(o[:, :], c1[:, :], t2m[:, :]),
                                 inc=svqr[p])
                            B.w("s", svqr[p])
                            if stq2.n > 0 and last_stq2[p] > 0:
                                B.w("s", stq2, last_stq2[p])
                            B.op("s", lambda e, i=qrb[p], o=q2b[p]:
                                 e.activation(o[:, :], i[:, :], AF.Square),
                                 inc=ssq2[p])
                            w_qr = svqr[p].n
                            w_svs = svs.n
                            w_q2 = ssq2[p].n
                            w_q2f = svq2f.n
                            sts.n += 1
                            stq2.n += 1
                            last_sts[p] = sts.n
                            last_stq2[p] = stq2.n

                            def mk_stats(p=p, w_qr=w_qr, w_svs=w_svs,
                                         w_q2=w_q2, w_q2f=w_q2f):
                                def f1(e):
                                    e.wait_ge(svqr[p].h, w_qr)
                                def f2(e):
                                    if w_svs > 0:
                                        e.wait_ge(svs.h, w_svs)
                                def f3(e, i=qrb[p]):
                                    e.matmul(pbf[3][:, :], ones128[:, :],
                                             i[:, :], start=True, stop=True
                                             ).then_inc(sts.h, 1)
                                def f4(e):
                                    e.wait_ge(ssq2[p].h, w_q2)
                                def f5(e):
                                    if w_q2f > 0:
                                        e.wait_ge(svq2f.h, w_q2f)
                                def f6(e, i=q2b[p]):
                                    e.matmul(pbf[4][:, :], ones128[:, :],
                                             i[:, :], start=True, stop=True
                                             ).then_inc(stq2.h, 1)
                                return [f1, f2, f3, f4, f5, f6]
                            pend_stats.extend(mk_stats())
                            B.w("v", sts, last_sts[p])
                            if ssstd[p].n > 0:
                                B.w("v", ssstd[p])
                            B.op("v", lambda e, b=pbf[3]:
                                 e.tensor_single_scalar(mtl[:, :], b[:, :],
                                                        1.0 / 128, ALU.mult),
                                 inc=svs)
                            B.w("v", stq2, last_stq2[p])
                            B.op("v", lambda e, b=pbf[4]:
                                 e.tensor_single_scalar(e2t[:, :], b[:, :],
                                                        1.0 / 128, ALU.mult),
                                 inc=svq2f)
                            B.w("v", svs)
                            B.op("v", lambda e:
                                 e.scalar_tensor_tensor(msqt[:, :], mtl[:, :],
                                                        1.0, mtl[:, :],
                                                        ALU.mult, ALU.mult))
                            B.op("v", lambda e, o=vart[p]:
                                 e.tensor_sub(o[:, :], e2t[:, :], msqt[:, :]),
                                 inc=svvar[p])
                            B.w("s", svvar[p])
                            if svt2b[p].n > 0:
                                B.w("s", svt2b[p])
                            B.op("s", lambda e, i=vart[p], o=stdt[p]:
                                 e.activation(o[:, :], i[:, :], AF.Ln,
                                              bias=epsv[:, :]))
                            B.op("s", lambda e, i=stdt[p], o=rstdt[p]:
                                 e.activation(o[:, :], i[:, :], AF.Exp,
                                              scale=-0.5), inc=ssstd[p])
                            B.w("v", ssstd[p])
                            if ssqp[p].n > 0:
                                B.w("v", ssqp[p])
                            B.op("v", lambda e, i=qrb[p]:
                                 e.tensor_sub(t1b[:, :], i[:, :], mtl[:, :]))
                            B.op("v", lambda e, o=t2b[p], r=rstdt[p]:
                                 e.tensor_mul(o[:, :], t1b[:, :], r[:, :]),
                                 inc=svt2b[p])
                            B.w("s", svt2b[p])
                            if dqwr[p].n > 0:
                                B.w("s", dqwr[p])
                            B.op("s", lambda e, i=t2b[p], o=qp[p]:
                                 e.activation(o[:, :], i[:, :], AF.Identity,
                                              bias=lnbi[:, :],
                                              scale=lnsc[:, :]), inc=ssqp[p])
                            B.w("g", ssqp[p])
                            B.dma("g", sdst[s, h, g], qp[p][:, :], dqwr[p])
                        flush_stats()
                        B.op("t", lambda e: e.matmul(
                            pbf[5][0:1, 0:1], onescol[:, :], onescol[:, :],
                            start=True, stop=True), inc=stprog)

            emit_pass("q")
            emit_pass("k")
            emit_pass("v")
            B.barrier([stprog, ssqp[0], ssqp[1], ssqp[2],
                       svt2b[0], svt2b[1], svt2b[2],
                       dqwr[0], dqwr[1], dqwr[2],
                       dln1w, dln1r, trf[0], trf[1]])

        with ExitStack() as p34:
            masks = _sb(p34, nc, "masks_sb", [128, 4 * TG], BF16)
            attnf = [_sb(p34, nc, f"af{i}", [128, T], BF16) for i in range(H)]

            dmk = B.sem("p3_dmk")
            B.dma("g", masks[:, :], masks_in, dmk)
            B.w("v", dmk, 16)
            dwo = B.sem("p3_dwo")

            dqkv = [B.sem("p0_ld0"), B.sem("p0_ld1")]
            stvt = [B.sem("p0_fr0"), B.sem("p0_fr1")]
            ssvt = [B.sem("p0_qw0"), B.sem("p0_qw1")]
            stS = [B.sem("p3_stS0"), B.sem("p3_stS1")]
            ssT = [B.sem("p3_ssT0"), B.sem("p3_ssT1")]
            svA = [B.sem("p3_svA0"), B.sem("p3_svA1")]
            stA = [B.sem("p3_stA0"), B.sem("p3_stA1")]
            sto = B.sem("p0_svch")
            stden = B.sem("p0_sgam")
            svden = B.sem("p0_svg")
            stdb = B.sem("p0_stg")
            ssrb = B.sem("p3_ssrb")
            svat = B.sem("p3_svat")
            sthdone = B.sem("p3_sthdone")
            sty = [B.sem("p4_sty0"), B.sem("p4_sty1")]
            ssyev = [B.sem("p4_ssyev0"), B.sem("p4_ssyev1")]
            xl2 = [B.sem("p4_xl20"), B.sem("p4_xl21")]
            svxn = [B.sem("p4_svxn0"), B.sem("p4_svxn1")]
            dxnw = [B.sem("p4_dxnw0"), B.sem("p4_dxnw1")]
            stp4 = B.sem("p4_stp4")

            B.w("t", B.sem("p1_trf0"))
            B.w("t", B.sem("p1_trf1"))

            for s in range(NSEQ):
              with ExitStack() as pA:
                qsb = [_sb(pA, nc, f"qsb{s}_{i}", [128, T], BF16) for i in (0, 1)]
                ksb = [_sb(pA, nc, f"ksb{s}_{i}", [128, T], BF16) for i in (0, 1)]
                vsb = [_sb(pA, nc, f"vsb{s}_{i}", [128, T], BF16) for i in (0, 1)]
                vtt = [_sb(pA, nc, f"vtt{s}_{i}", [128, T], BF16) for i in (0, 1)]
                Ttl = [_sb(pA, nc, f"Ttl{s}_{i}", [128, TG], F32) for i in (0, 1)]
                Abf = [_sb(pA, nc, f"Abf{s}_{i}", [128, TG], BF16) for i in (0, 1)]
                drb = [_sb(pA, nc, f"drb{s}_{i}", [1, TG], BF16) for i in (0, 1)]
                rbt = [_sb(pA, nc, f"rbt{s}_{i}", [128, TG], BF16) for i in (0, 1)]
                if s > 0:
                    B.barrier([stp4, svxn[0], svxn[1], dxnw[0], dxnw[1],
                               rln.ssa, ssyev[0], ssyev[1], sty[0], sty[1]])
                B.w("v", stp4)
                for h in range(H):
                      p = h % 2
                      if dqkv[p].n > 0:
                          B.w("sy", sthdone)
                      B.dma("sy", qsb[p][:, :].rearrange("p (g w) -> p g w", g=G),
                            s_q[s, h].rearrange("g p w -> p g w"), dqkv[p])
                      B.dma("sy", ksb[p][:, :].rearrange("p (g w) -> p g w", g=G),
                            s_k[s, h].rearrange("g p w -> p g w"), dqkv[p])
                      B.dma("sy", vsb[p][:, :].rearrange("p (g w) -> p g w", g=G),
                            s_v[s, h].rearrange("g p w -> p g w"), dqkv[p])
                      B.w("t", dqkv[p])
                      if sthdone.n >= 2:
                          B.w("s", sthdone)
                      for c in range(16):
                          q = c % 2
                          if stvt[q].n > 0:
                              B.w("t", ssvt[q])
                          B.op("t", lambda e, i=vsb[p], c_=c, b=pbt[q]:
                               e.transpose(b[:, 0:128],
                                           i[:, c_ * 128:(c_ + 1) * 128],
                                           ident[:, :]), inc=stvt[q])
                          B.w("s", stvt[q])
                          B.op("s", lambda e, b=pbt[q], o=vtt[p], c_=c:
                               e.activation(o[:, c_ * 128:(c_ + 1) * 128],
                                            b[:, 0:128], AF.Copy), inc=ssvt[q])
                      B.w("t", ssvt[0])
                      B.w("t", ssvt[1])
                      for g in range(G):
                          nch = 4 * g + 4

                          def emit_S(c):
                              cp = c % 2
                              if stS[cp].n > 0:
                                  B.w("t", ssT[cp])
                              B.op("t", lambda e, b=pbf[cp], i=ksb[p], c_=c,
                                   j=qsb[p], a=g * TG:
                                   e.matmul(b[:, :],
                                            i[:, c_ * 128:(c_ + 1) * 128],
                                            j[:, a:a + TG],
                                            start=True, stop=True), inc=stS[cp])
                              B.w("s", stS[cp])
                              if svA[cp].n > 0:
                                  B.w("s", svA[cp])
                              B.op("s", lambda e, b=pbf[cp], o=Ttl[cp]:
                                   e.activation(o[:, :], b[:, :], AF.Square),
                                   inc=ssT[cp])
                              B.w("v", ssT[cp])
                              if stA[cp].n > 0:
                                  B.w("v", stA[cp])
                              if c >= 4 * g:
                                  j = c - 4 * g
                                  B.op("v", lambda e, i=Ttl[cp], o=Abf[cp]:
                                       e.tensor_mul(o[:, :], i[:, :], i[:, :]))
                                  B.op("v", lambda e, o=Abf[cp], j_=j:
                                       e.tensor_mul(o[:, :], o[:, :],
                                                    masks[:, j_ * TG:(j_ + 1) * TG]),
                                       inc=svA[cp])
                              else:
                                  B.op("v", lambda e, i=Ttl[cp], o=Abf[cp]:
                                       e.tensor_mul(o[:, :], i[:, :], i[:, :]),
                                       inc=svA[cp])

                          def emit_AVden(c):
                              cp = c % 2
                              if c == 0:
                                  B.w("t", svat)
                              B.w("t", svA[cp])
                              B.op("t", lambda e, b=pbf[2], i=vtt[p], c_=c,
                                   a=Abf[cp], st=(c == 0), sp=(c == nch - 1):
                                   e.matmul(b[:, :],
                                            i[:, c_ * 128:(c_ + 1) * 128],
                                            a[:, :], start=st, stop=sp),
                                   inc=(sto if c == nch - 1 else None))
                              B.op("t", lambda e, b=pbf[3], a=Abf[cp],
                                   st=(c == 0), sp=(c == nch - 1):
                                   e.matmul(b[0:1, :], onescol[:, :], a[:, :],
                                            start=st, stop=sp),
                                   inc=(stden if c == nch - 1 else stA[cp]))

                          for c in range(nch):
                              emit_S(c)
                              if c >= 1:
                                  emit_AVden(c - 1)
                          emit_AVden(nch - 1)
                          B.w("v", stden)
                          if svden.n >= 2:
                              B.w("v", stdb)
                          B.op("v", lambda e, b=pbf[3], o=drb[g % 2]:
                               e.reciprocal(o[:, :], b[0:1, :]), inc=svden)
                          B.w("t", svden)
                          B.w("t", ssrb)
                          B.op("t", lambda e, b=pbf[4], i=drb[g % 2]:
                               e.matmul(b[:, :], onesrow[:, :], i[:, :],
                                        start=True, stop=True), inc=stdb)
                          B.w("s", stdb)
                          if ssrb.n >= 2:
                              B.w("s", svat)
                          B.op("s", lambda e, b=pbf[4], o=rbt[g % 2]:
                               e.activation(o[:, :], b[:, :], AF.Copy), inc=ssrb)
                          B.w("v", sto)
                          B.w("v", ssrb)
                          B.op("v", lambda e, b=pbf[2], i=rbt[g % 2], o=attnf[h],
                               a=g * TG:
                               e.tensor_mul(o[:, a:a + TG], b[:, :], i[:, :]),
                               inc=svat)
                      B.op("t", lambda e: e.matmul(
                          pbf[5][0:1, 0:1], onescol[:, :], onescol[:, :],
                          start=True, stop=True), inc=sthdone)

              with ExitStack() as pB:
                woutb = [_sb(pB, nc, f"wo{s}_{i}", [128, 2048], BF16)
                         for i in range(FC)]
                yf = [_sb(pB, nc, f"yf{s}_{i}", [128, D], F32) for i in (0, 1)]
                lny = [_sb(pB, nc, f"lny{s}_{i}", [128, D], F32) for i in (0, 1)]
                xt2 = [_sb(pB, nc, f"xt2_{s}_{i}", [128, D], F32) for i in (0, 1)]
                xnw = [_sb(pB, nc, f"xnw{s}_{i}", [128, D], F32) for i in (0, 1)]
                B.barrier([sthdone, svat, ssT[0], ssT[1], ssrb,
                           ssvt[0], ssvt[1]])
                for fc in range(FC):
                    B.dma("g", woutb[fc][:, :],
                          wout[fc * 128:(fc + 1) * 128, :], dwo)
                B.w("t", dwo)
                B.w("t", svat)
                for ti in range(16):
                    trow = s * T + ti * 128
                    pp = ti % 2
                    for jg in range(4):
                        pj = jg % 2
                        if sty[pj].n > 0:
                            B.w("t", ssyev[pj])
                        for h in range(H):
                            B.op("t", lambda e, b=pbf[pj], i=attnf[h], t_=ti,
                                 hh=h, jj=jg, st=(h == 0), sp=(h == H - 1):
                                 e.matmul(b[:, :],
                                          i[:, t_ * 128:(t_ + 1) * 128],
                                          woutb[hh][:, jj * TG:(jj + 1) * TG],
                                          start=st, stop=sp),
                                 inc=(sty[pj] if h == H - 1 else None))
                        B.w("s", sty[pj])
                        if ti >= 2 and jg == 0:
                            B.w("s", rln.ssa)
                        B.op("s", lambda e, b=pbf[pj], o=yf[pp], jj=jg:
                             e.activation(o[:, jj * TG:(jj + 1) * TG], b[:, :],
                                          AF.Copy), inc=ssyev[pj])
                    if xl2[pp].n > 0:
                        B.w("sy", svxn[pp])
                    B.dma("sy", xt2[pp][:, :], x_in[trow:trow + 128, :],
                          xl2[pp])
                    rln.emit(yf[pp][:, :], yf[pp][:, :], lny[pp][:, :], D, epsv,
                             act_prewaits=[(ssyev[0], ssyev[0].n),
                                           (ssyev[1], ssyev[1].n),
                                           (svxn[pp], svxn[pp].n)],
                             dve_prewaits=[(ssyev[0], ssyev[0].n),
                                           (ssyev[1], ssyev[1].n)])
                    B.w("v", rln.ssa)
                    B.w("v", xl2[pp])
                    if dxnw[pp].n > 0:
                        B.w("v", dxnw[pp])
                    B.op("v", lambda e, o=xnw[pp], a=lny[pp], b2=xt2[pp]:
                         e.tensor_add(o[:, :], a[:, :], b2[:, :]),
                         inc=svxn[pp])
                    B.w("g", svxn[pp])
                    B.dma("g", s_xnew[trow:trow + 128, :], xnw[pp][:, :],
                          dxnw[pp])
                B.op("t", lambda e: e.matmul(
                    pbf[5][0:1, 0:1], onescol[:, :], onescol[:, :],
                    start=True, stop=True), inc=stp4)
            B.barrier([stp4, svat, svxn[0], svxn[1], dxnw[0], dxnw[1],
                       sthdone, ssvt[0], ssvt[1], rln.ssa])

        with ExitStack() as p67:
            ln2f = [_sb(p67, nc, f"ln2f{i}", [128, NSEQ * T], BF16)
                    for i in range(FC)]
            xt3 = [_sb(p67, nc, f"xt3_{i}", [128, D], F32) for i in (0, 1)]
            ln2t = [_sb(p67, nc, f"ln2t{i}", [128, D], BF16) for i in (0, 1)]

            xl3 = [B.sem("p1_xl0"), B.sem("p1_xl1")]
            trw2 = [B.sem("p1_trw0"), B.sem("p1_trw1")]
            trf2 = [B.sem("p1_trf0"), B.sem("p1_trf1")]

            for i in range(NSEQ * T // 128):
                p = i % 2
                if xl3[p].n > 0:
                    B.w("sy", rln.ssa, rln.ssa.n)
                B.dma("sy", xt3[p][:, :], s_xnew[i * 128:(i + 1) * 128, :],
                      xl3[p])
                if i >= 2:
                    B.w("s", trw2[0])
                    B.w("s", trw2[1])
                rln.emit(xt3[p][:, :], xt3[p][:, :], ln2t[p][:, :], D, epsv,
                         act_prewaits=[(xl3[p], xl3[p].n)],
                         dve_prewaits=[(xl3[p], xl3[p].n)])
                B.w("t", rln.ssa)
                for fc in range(FC):
                    q = fc % 2
                    if trw2[q].n > 0:
                        B.w("t", trf2[q])
                    B.op("t", lambda e, i_=ln2t[p], f=fc, b=pbt[q]:
                         e.transpose(b[:, 0:128], i_[:, f * 128:(f + 1) * 128],
                                     ident[:, :]), inc=trw2[q])
                    B.w("v", trw2[q])
                    B.op("v", lambda e, b=pbt[q], f=fc, o=i * 128:
                         e.tensor_copy(ln2f[f][:, o:o + 128], b[:, 0:128]),
                         inc=trf2[q])

            w1c = [_sb(p67, nc, f"w1c{i}", [128, 2048], BF16) for i in (0, 1)]
            w2c = [_sb(p67, nc, f"w2c{i}", [128, 2048], BF16) for i in (0, 1)]
            sgc1 = _sb(p67, nc, "sgc", [128, NSEQ * T], BF16)
            sgc = [sgc1, sgc1]
            hdc1 = _sb(p67, nc, "hdc", [128, NSEQ * T], BF16)
            hdc = [hdc1, hdc1]

            wc1l = [B.sem("p0_ld0"), B.sem("p0_ld1")]
            wc2l = [B.sem("dC"), B.sem("p1_dcs")]
            stx1 = B.sem("p1_sts0")
            stx2 = B.sem("p1_stq20")
            sssg = B.sem("p1_ssq20")
            svhd = B.sem("p1_svqr0")
            dhw = [B.sem("p1_dqwr0"), B.sem("p1_dqwr1")]
            stw1f = [B.sem("p1_stpr0"), B.sem("p1_stpr1")]

            B.w("t", trf2[0])
            B.w("t", trf2[1])
            NB = NSEQ * T // TG
            for oc in range(OC):
                p = oc % 2
                if wc1l[p].n > 0:
                    B.w("sy", stw1f[p])
                B.dma("sy", w1c[p][:, :].rearrange("p (fc o) -> p fc o", fc=FC),
                      s_w1q[oc].rearrange("fc p o -> p fc o"), wc1l[p])
                B.dma("sy", w2c[p][:, :].rearrange("p (fc o) -> p fc o", fc=FC),
                      s_w2q[oc].rearrange("fc p o -> p fc o"), wc2l[p])
                B.w("t", wc1l[p])
                B.w("t", wc2l[p])
                for b in range(NB):
                    bb = b % 2
                    if stx1.n >= 2:
                        B.w("t", sssg)
                    for fc in range(FC):
                        B.op("t", lambda e, f=fc, bk=pbf[bb], w=w1c[p], b_=b,
                             st=(fc == 0), sp=(fc == FC - 1):
                             e.matmul(bk[:, :], w[:, f * 128:(f + 1) * 128],
                                      ln2f[f][:, b_ * TG:(b_ + 1) * TG],
                                      start=st, stop=sp),
                             inc=(stx1 if fc == FC - 1 else None))
                    if stx2.n >= 2:
                        B.w("t", svhd)
                    for fc in range(FC):
                        B.op("t", lambda e, f=fc, bk=pbf[2 + bb], w=w2c[p],
                             b_=b, st=(fc == 0), sp=(fc == FC - 1):
                             e.matmul(bk[:, :], w[:, f * 128:(f + 1) * 128],
                                      ln2f[f][:, b_ * TG:(b_ + 1) * TG],
                                      start=st, stop=sp),
                             inc=(stx2 if fc == FC - 1 else None))
                    B.w("s", stx1)
                    if oc >= 1 and b == 0:
                        B.w("s", svhd)
                    B.op("s", lambda e, bk=pbf[bb], o=sgc[p], b_=b:
                         e.activation(o[:, b_ * TG:(b_ + 1) * TG], bk[:, :],
                                      AF.Sigmoid), inc=sssg)
                    B.w("v", sssg)
                    B.w("v", stx2)
                    if b == 0:
                        B.w("v", dhw[0])
                        B.w("v", dhw[1])
                    B.op("v", lambda e, bk=pbf[2 + bb], i=sgc[p], o=hdc[p],
                         b_=b:
                         e.tensor_mul(o[:, b_ * TG:(b_ + 1) * TG],
                                      i[:, b_ * TG:(b_ + 1) * TG], bk[:, :]),
                         inc=svhd)
                B.w("g", svhd)
                B.dma("g", s_hid[:, :, oc].rearrange("tb p t -> p tb t"),
                      hdc[p][:, :].rearrange("p (tb t) -> p tb t", tb=8),
                      dhw[p])
                B.op("t", lambda e: e.matmul(
                    pbf[5][0:1, 0:1], onescol[:, :], onescol[:, :],
                    start=True, stop=True), inc=stw1f[p])
            B.barrier([stw1f[0], stw1f[1], dhw[0], dhw[1], svhd, sssg,
                       trf2[0], trf2[1], rln.ssa])

        with ExitStack() as p8:
            TB = 512
            NTB = NSEQ * T // TB
            hidsl = _sb(p8, nc, "hidsl", [128, OC * TB], BF16)
            w3sl = [_sb(p8, nc, f"w3sl{i}", [128, OC * TG], BF16) for i in (0, 1)]
            y3b = [_sb(p8, nc, f"y3b{i}", [128, D], BF16) for i in range(TB // 128)]
            lny3 = [_sb(p8, nc, f"lny3{i}", [128, D], F32) for i in (0, 1)]
            xt4 = [_sb(p8, nc, f"xt4_{i}", [128, D], F32) for i in (0, 1)]
            outt = [_sb(p8, nc, f"outt{i}", [128, D], F32) for i in (0, 1)]

            hbl = B.sem("p1_dln1r")
            w3l = [B.sem("p1_xl0"), B.sem("p1_xl1")]
            stx3 = B.sem("p3_stS0")
            ssy3 = B.sem("p3_ssT0")
            xl4 = [B.sem("p4_xl20"), B.sem("p4_xl21")]
            svout = [B.sem("p4_svxn0"), B.sem("p4_svxn1")]
            dy = B.sem("p8_dy")
            stblk = B.sem("p3_sthdone")

            jg_marks = {}
            w3_loaded = {}

            def emit_w3_load(gj):
                if gj in w3_loaded or gj >= 4 * NTB:
                    return
                pj = gj % 2
                if gj >= 2:
                    B.w("sy", stblk, jg_marks[gj - 2])
                jg_ = gj % 4
                B.dma("sy",
                      w3sl[pj][:, :].rearrange("p (hc j) -> p hc j", hc=OC),
                      s_w3q[jg_], w3l[pj])
                w3_loaded[gj] = w3l[pj].n

            for tb in range(NTB):
                if tb > 0:
                    B.w("s", stblk)
                B.dma("s",
                      hidsl[:, :].rearrange("p (hc t) -> p hc t", hc=OC),
                      s_hid[tb], hbl)
                for jg in range(4):
                    gj = tb * 4 + jg
                    emit_w3_load(gj)
                    emit_w3_load(gj + 1)
                    pj = gj % 2
                    B.w("t", hbl)
                    B.w("t", w3l[pj], w3_loaded[gj])
                    for tt in range(TB // 128):
                        bb = tt % 2
                        if stx3.n >= 2:
                            B.w("t", ssy3)
                        for hc in range(OC):
                            B.op("t", lambda e, b_=pbf[bb], h_=hc, t_=tt,
                                 pj_=pj, st=(hc == 0), sp=(hc == OC - 1):
                                 e.matmul(b_[:, :],
                                          hidsl[:, h_ * TB + t_ * 128:
                                                h_ * TB + (t_ + 1) * 128],
                                          w3sl[pj_][:, h_ * TG:(h_ + 1) * TG],
                                          start=st, stop=sp),
                                 inc=(stx3 if hc == OC - 1 else None))
                        B.w("s", stx3)
                        B.op("s", lambda e, b_=pbf[bb], o=y3b[tt], jj=jg:
                             e.activation(o[:, jj * TG:(jj + 1) * TG], b_[:, :],
                                          AF.Copy), inc=ssy3)
                    B.op("t", lambda e: e.matmul(
                        pbf[5][0:1, 0:1], onescol[:, :], onescol[:, :],
                        start=True, stop=True), inc=stblk)
                    jg_marks[gj] = stblk.n
                for tt in range(TB // 128):
                    pp = tt % 2
                    row = tb * TB + tt * 128
                    if xl4[pp].n > 0:
                        B.w("sy", svout[pp])
                    B.dma("sy", xt4[pp][:, :], s_xnew[row:row + 128, :],
                          xl4[pp])
                    rln.emit(y3b[tt][:, :], y3b[tt][:, :], lny3[pp][:, :], D,
                             epsv,
                             act_prewaits=[(ssy3, ssy3.n),
                                           (svout[pp], svout[pp].n)],
                             dve_prewaits=[(ssy3, ssy3.n)])
                    B.w("v", rln.ssa)
                    B.w("v", xl4[pp])
                    if dy.n >= 32:
                        B.w("v", dy, dy.n - 16)
                    B.op("v", lambda e, o=outt[pp], a=lny3[pp], b2=xt4[pp]:
                         e.tensor_add(o[:, :], a[:, :], b2[:, :]),
                         inc=svout[pp])
                    B.w("sy", svout[pp])
                    B.dma("sy", y_out[row:row + 128, :], outt[pp][:, :], dy)
            B.w("sy", dy)

        with nc.allow_low_precision(
                "bf16 matmul operands / bf16 rounding blessed for this problem"
        ), nc.Block() as block:
            @block.sync
            def _(e):
                for f in B.prog["sy"]:
                    f(e)

            @block.gpsimd
            def _(e):
                for f in B.prog["g"]:
                    f(e)

            @block.tensor
            def _(e):
                for f in B.prog["t"]:
                    f(e)

            @block.scalar
            def _(e):
                for f in B.prog["s"]:
                    f(e)

            @block.vector
            def _(e):
                for f in B.prog["v"]:
                    f(e)

    return nc


def kernel(L0_attn_x, L0_attn_w_qkv, L0_attn_w_out, L0_attn_ln_scale,
           L0_attn_ln_bias, L0_ffn_w1, L0_ffn_w2, L0_ffn_w3):
    x = np.ascontiguousarray(np.asarray(L0_attn_x, dtype=np.float32))
    wqkv_a = np.ascontiguousarray(np.asarray(L0_attn_w_qkv, dtype=np.float32))
    wout_a = np.ascontiguousarray(np.asarray(L0_attn_w_out, dtype=np.float32))
    lnsc_a = np.ascontiguousarray(
        np.asarray(L0_attn_ln_scale, dtype=np.float32).reshape(128, 1))
    lnbi_a = np.ascontiguousarray(
        np.asarray(L0_attn_ln_bias, dtype=np.float32).reshape(128, 1))
    w1 = np.asarray(L0_ffn_w1, dtype=np.float32)
    w2 = np.asarray(L0_ffn_w2, dtype=np.float32)
    w3 = np.asarray(L0_ffn_w3, dtype=np.float32)

    w1tt_a = np.ascontiguousarray(
        w1.reshape(OC, 128, FC, 128).transpose(0, 2, 3, 1))
    w2tt_a = np.ascontiguousarray(
        w2.reshape(OC, 128, FC, 128).transpose(0, 2, 3, 1))
    w3t_a = np.ascontiguousarray(w3.T)

    pos = np.arange(T, dtype=np.float64)
    inv = 10000.0 ** (-np.arange(0, HD, 2, dtype=np.float64) / HD)
    ang = pos[:, None] * inv[None, :]
    c = np.cos(ang).astype(np.float32).T
    sn = np.sin(ang).astype(np.float32).T
    cosf_a = np.ascontiguousarray(np.concatenate([c, c], axis=0))
    sinf_a = np.ascontiguousarray(np.concatenate([-sn, sn], axis=0))
    ident_a = np.eye(128, dtype=np.float32)
    tk = np.arange(128)[:, None]
    tq = np.arange(TG)[None, :]
    masks_a = np.ascontiguousarray(
        np.stack([(tk <= tq - 128 * j).astype(np.float32) for j in range(4)],
                 axis=1).reshape(128, 4 * TG))

    nc = build_program()
    in_maps = []
    for core in range(NCORES):
        xs = np.ascontiguousarray(
            x[2 * core:2 * core + 2].reshape(NSEQ * T, D))
        in_maps.append({
            "x": xs, "wqkv": wqkv_a, "wout": wout_a,
            "lnsc": lnsc_a, "lnbi": lnbi_a,
            "w1tt": w1tt_a, "w2tt": w2tt_a, "w3t": w3t_a,
            "cosf": cosf_a, "sinf": sinf_a, "identm": ident_a,
            "masks": masks_a,
        })
    res = run_bass_kernel_spmd(nc, in_maps, core_ids=list(range(NCORES)))
    y = np.concatenate(
        [r["y"].reshape(NSEQ, T, D) for r in res.results], axis=0)
    return y


# version 12 (final)
# speedup vs baseline: 1.0344x; 1.0344x over previous
import numpy as np
import concourse.bass as bass
import concourse.mybir as mybir
from concourse.bass_utils import run_bass_kernel_spmd
from contextlib import ExitStack

F32 = mybir.dt.float32
BF16 = mybir.dt.bfloat16
AF = mybir.ActivationFunctionType
ALU = mybir.AluOpType
AX = mybir.AxisListType

NCORES = 8
NSEQ = 2
T = 2048
D = 2048
H = 16
HD = 128
G = 4
TG = 512
FC = 16
FFH = 5632
OC = FFH // 128
LNEPS = 1e-5
QEPS = 1e-8
ENGS = ("sy", "g", "t", "s", "v")
DBG = []


class Sem:
    __slots__ = ("h", "n")

    def __init__(self, nc, name):
        self.h = nc.alloc_semaphore(name=name)
        self.n = 0


class Builder:
    def __init__(self, nc):
        self.nc = nc
        self.prog = {k: [] for k in ENGS}
        self._sems = {}

    def sem(self, name):
        if name not in self._sems:
            self._sems[name] = Sem(self.nc, name)
        return self._sems[name]

    def w(self, eng, s, val=None):
        v = s.n if val is None else val
        if v <= 0:
            return
        self.prog[eng].append(lambda e, h=s.h, v=v: e.wait_ge(h, v))

    def op(self, eng, fn, inc=None, k=1, inc2=None, k2=1):
        if inc is not None:
            inc.n += k
        if inc2 is not None:
            inc2.n += k2
        def run(e, fn=fn, inc=inc, k=k, inc2=inc2, k2=k2):
            ins = fn(e)
            if inc is not None:
                ins = ins.then_inc(inc.h, k)
            if inc2 is not None:
                ins.then_inc(inc2.h, k2)
        self.prog[eng].append(run)

    def dma(self, eng, out_ap, in_ap, inc):
        self.op(eng, lambda e, o=out_ap, i=in_ap: e.dma_start(o, i), inc, 16)

    def barrier(self, marks, engs=ENGS):
        for eng in engs:
            for s in marks:
                self.w(eng, s, s.n)


def _sb(ctx, nc, name, shape, dtype):
    return ctx.enter_context(nc.sbuf_tensor(name, shape, dtype))


class RowLN:

    def __init__(self, B, ctx, tag):
        nc = B.nc
        self.B = B
        self.junk = _sb(ctx, nc, f"{tag}_junk", [128, 2048], F32)
        self.sums = _sb(ctx, nc, f"{tag}_sums", [128, 1], F32)
        self.mean = _sb(ctx, nc, f"{tag}_mean", [128, 1], F32)
        self.msq = _sb(ctx, nc, f"{tag}_msq", [128, 1], F32)
        self.ssq = [_sb(ctx, nc, f"{tag}_ssq{i}", [128, 1], F32) for i in (0, 1)]
        self.var = [_sb(ctx, nc, f"{tag}_var{i}", [128, 1], F32) for i in (0, 1)]
        self.std = [_sb(ctx, nc, f"{tag}_std{i}", [128, 1], F32) for i in (0, 1)]
        self.rstd = [_sb(ctx, nc, f"{tag}_rstd{i}", [128, 1], F32) for i in (0, 1)]
        self.nmr = [_sb(ctx, nc, f"{tag}_nmr{i}", [128, 1], F32) for i in (0, 1)]
        self.ss1 = B.sem(f"{tag}_ss1")
        self.sv1 = B.sem(f"{tag}_sv1")
        self.ss2 = B.sem(f"{tag}_ss2")
        self.sv2 = B.sem(f"{tag}_sv2")
        self.ssa = B.sem(f"{tag}_ssa")
        self.svx = B.sem(f"{tag}_svx")
        self.uses = 0

    def emit(self, src_ap, apply_src_ap, out_ap, fd, epsap,
             act_prewaits=(), dve_prewaits=()):
        B = self.B
        p = self.uses % 2
        self.uses += 1
        if self.uses > 2:
            B.w("s", self.sv1)
            B.w("s", self.sv2)
            B.w("v", self.ss2)
            B.w("v", self.ssa)
        for s_, v_ in act_prewaits:
            B.w("s", s_, v_)
        for s_, v_ in dve_prewaits:
            B.w("v", s_, v_)
        B.op("s", lambda e, s=src_ap, j=self.junk, q=self.ssq[p], f=fd:
             e.activation(j[:, 0:f], s, AF.Square, accum_out=q[:, :]),
             inc=self.ss1)
        B.op("v", lambda e, s=src_ap, o=self.sums:
             e.reduce_sum(o[:, :], s, AX.X), inc=self.svx)
        B.w("v", self.svx)
        B.op("v", lambda e, i=self.sums, o=self.mean, c=1.0 / fd:
             e.tensor_single_scalar(o[:, :], i[:, :], c, ALU.mult),
             inc=self.svx)
        B.w("v", self.svx)
        B.op("v", lambda e, m=self.mean, o=self.msq:
             e.scalar_tensor_tensor(o[:, :], m[:, :], 1.0, m[:, :],
                                    ALU.mult, ALU.mult), inc=self.svx)
        B.w("v", self.ss1)
        B.w("v", self.svx)
        B.op("v", lambda e, q=self.ssq[p], m=self.msq, o=self.var[p], c=1.0 / fd:
             e.scalar_tensor_tensor(o[:, :], q[:, :], c, m[:, :],
                                    ALU.mult, ALU.subtract), inc=self.sv1)
        B.w("s", self.sv1)
        B.op("s", lambda e, i=self.var[p], o=self.std[p], b=epsap:
             e.activation(o[:, :], i[:, :], AF.Sqrt, bias=b[:, :]),
             inc=self.ss2)
        B.w("v", self.ss2)
        B.op("v", lambda e, i=self.std[p], o=self.rstd[p]:
             e.reciprocal(o[:, :], i[:, :]), inc=self.svx)
        B.w("v", self.svx)
        B.op("v", lambda e, m=self.mean, r=self.rstd[p], o=self.nmr[p]:
             e.scalar_tensor_tensor(o[:, :], m[:, :], -1.0, r[:, :],
                                    ALU.mult, ALU.mult), inc=self.sv2)
        B.w("s", self.sv2)
        B.op("s", lambda e, i=apply_src_ap, o=out_ap, r=self.rstd[p],
             n=self.nmr[p]:
             e.activation(o, i, AF.Identity, bias=n[:, :], scale=r[:, :]),
             inc=self.ssa)


def build_program():
    nc = bass.Bass("TRN2", target_bir_lowering=False)
    B = Builder(nc)

    x_in = nc.dram_tensor("x", [NSEQ * T, D], F32, kind="ExternalInput").ap()
    wqkv = nc.dram_tensor("wqkv", [D, 3 * D], F32, kind="ExternalInput").ap()
    wout = nc.dram_tensor("wout", [D, D], F32, kind="ExternalInput").ap()
    lnsc_in = nc.dram_tensor("lnsc", [128, 1], F32, kind="ExternalInput").ap()
    lnbi_in = nc.dram_tensor("lnbi", [128, 1], F32, kind="ExternalInput").ap()
    w1tt = nc.dram_tensor("w1tt", [OC, FC, 128, 128], F32, kind="ExternalInput").ap()
    w2tt = nc.dram_tensor("w2tt", [OC, FC, 128, 128], F32, kind="ExternalInput").ap()
    w3t = nc.dram_tensor("w3t", [FFH, D], F32, kind="ExternalInput").ap()
    cosf_in = nc.dram_tensor("cosf", [128, T], F32, kind="ExternalInput").ap()
    sinf_in = nc.dram_tensor("sinf", [128, T], F32, kind="ExternalInput").ap()
    ident_in = nc.dram_tensor("identm", [128, 128], F32, kind="ExternalInput").ap()
    masks_in = nc.dram_tensor("masks", [128, 4 * TG], F32, kind="ExternalInput").ap()

    y_out = nc.dram_tensor("y", [NSEQ * T, D], F32, kind="ExternalOutput").ap()

    def scr(name, shape, dtype):
        kind = "ExternalOutput" if name in DBG else "Internal"
        return nc.dram_tensor(name, shape, dtype, kind=kind).ap()

    s_ln1f = scr("s_ln1f", [NSEQ, G, FC, 128, TG], BF16)
    s_q = scr("s_q", [NSEQ, H, G, 128, TG], BF16)
    s_k = scr("s_k", [NSEQ, H, G, 128, TG], BF16)
    s_v = scr("s_v", [NSEQ, H, G, 128, TG], BF16)
    s_xnew = scr("s_xnew", [NSEQ * T, D], F32)
    s_hid = scr("s_hid", [NSEQ * T // 512, OC, 128, 512], BF16)
    s_w1q = scr("s_w1q", [OC, FC, 128, 128], BF16)
    s_w2q = scr("s_w2q", [OC, FC, 128, 128], BF16)
    s_w3q = scr("s_w3q", [OC, 4, 128, TG], BF16)

    top = ExitStack()
    with top:
        ident = _sb(top, nc, "ident", [128, 128], BF16)
        ones128 = _sb(top, nc, "ones128", [128, 128], BF16)
        onescol = _sb(top, nc, "onescol", [128, 1], BF16)
        onesrow = _sb(top, nc, "onesrow", [1, 128], BF16)
        onesf = _sb(top, nc, "onesf", [128, 128], F32)
        lnsc = _sb(top, nc, "lnsc_t", [128, 1], F32)
        lnbi = _sb(top, nc, "lnbi_t", [128, 1], F32)
        epsv = _sb(top, nc, "epsv", [128, 1], F32)
        tau = [_sb(top, nc, f"tau{i}", [128, 1], F32) for i in range(3)]
        ntau = [_sb(top, nc, f"ntau{i}", [128, 1], F32) for i in range(3)]
        gpart = _sb(top, nc, "gpart", [128, 136], F32)

        pbf = [top.enter_context(nc.psum_tensor(f"pbf{i}", [128, 512], F32))
               for i in range(6)]
        pbt = [top.enter_context(nc.psum_tensor(f"pbt{i}", [128, 1024], BF16))
               for i in range(2)]

        rln = RowLN(B, top, "rln")

        dC = B.sem("dC")
        dG = B.sem("dG")
        B.op("g", lambda e: e.memset(ones128[:, :], 1.0), inc=dG)
        B.op("g", lambda e: e.memset(onescol[:, :], 1.0), inc=dG)
        B.op("g", lambda e: e.memset(onesrow[:, :], 1.0), inc=dG)
        B.op("g", lambda e: e.memset(onesf[:, :], 1.0), inc=dG)
        B.op("g", lambda e: e.memset(epsv[:, :], LNEPS), inc=dG)
        dI = B.sem("dI")
        B.dma("g", ident[:, :], ident_in, dI)
        B.dma("sy", lnsc[:, :], lnsc_in, dC)
        B.dma("sy", lnbi[:, :], lnbi_in, dC)
        B.w("t", dG, 6)
        B.w("t", dI, 16)
        B.w("s", dG, 6)
        B.w("v", dG, 6)
        B.w("s", dC, 32)
        cneg = _sb(top, nc, "cneg", [128, 1], F32)
        nc.const_aps.register(F32, -0.5, cneg[:, :])
        B.op("g", lambda e: e.memset(cneg[:, :], -0.5), inc=dG)

        with ExitStack() as p0:
            CW = 4096
            wcb = [_sb(p0, nc, f"p0_w{i}", [128, CW], F32) for i in (0, 1)]
            lsb = [_sb(p0, nc, f"p0_l{i}", [128, CW], F32) for i in (0, 1)]
            gjk = [_sb(p0, nc, f"p0_gjk{i}", [128, CW], F32) for i in (0, 1)]
            qb = [_sb(p0, nc, f"p0_q{i}", [128, CW], BF16) for i in (0, 1)]

            ld = [B.sem("p0_ld0"), B.sem("p0_ld1")]
            fr = [B.sem("p0_fr0"), B.sem("p0_fr1")]
            qwr = [B.sem("p0_qw0"), B.sem("p0_qw1")]
            svch = B.sem("p0_svch")
            sgch = B.sem("p0_sgch")
            sgam = B.sem("p0_sgam")
            svg = B.sem("p0_svg")
            stg = B.sem("p0_stg")

            def w12_src(arr):
                return lambda c, a=arr: (
                    a[2 * c:2 * c + 2].rearrange("oc fc p o -> p oc fc o"))

            def w12_dst(arr):
                return lambda c, a=arr: (
                    a[2 * c:2 * c + 2].rearrange("oc fc p o -> p oc fc o"))

            def w3_src(c):
                return w3t.rearrange("(hc p) j -> p hc j", p=128)[
                    :, 2 * c:2 * c + 2, :]

            def w3_dst(c):
                return s_w3q[2 * c:2 * c + 2].rearrange(
                    "hc jg p j -> p hc jg j")

            mats = [
                ("w1", OC // 2, w12_src(w1tt), w12_dst(s_w1q), 3),
                ("w2", OC // 2, w12_src(w2tt), w12_dst(s_w2q), 3),
                ("w3", OC // 2, w3_src, w3_dst, 2),
            ]

            def buf_view(b, nd, split_j=False):
                if nd == 3:
                    return b[:, :].rearrange("p (oc fc o) -> p oc fc o",
                                             oc=2, fc=FC)
                if split_j:
                    return b[:, :].rearrange("p (hc jg j) -> p hc jg j",
                                             hc=2, jg=4)
                return b[:, :].rearrange("p (hc j) -> p hc j", hc=2)

            col = 0
            gcols = {}
            for mi, (name, nch, srcf, dstf, nd) in enumerate(mats):
                gcols[name] = (col, nch)
                for c in range(nch):
                    p = c % 2
                    if ld[p].n > 0:
                        B.w("sy", fr[p])
                    B.dma("sy", buf_view(wcb[p], nd), srcf(c), ld[p])
                    if p == 0:
                        B.w("s", ld[p])
                        B.w("s", fr[p])
                        B.op("s", lambda e, b=wcb[p], o=gpart, cc=col:
                             e.activation(gjk[0][:, :], b[:, :], AF.Abs,
                                          accum_out=o[:, cc:cc + 1]),
                             inc=fr[p])
                    else:
                        B.w("v", ld[p])
                        B.op("v", lambda e, b=wcb[p], o=gpart, cc=col:
                             e.reduce_sum(o[:, cc:cc + 1], b[:, :], AX.X,
                                          apply_absolute_value=True),
                             inc=fr[p])
                    col += 1
            B.op("s", lambda e: e.activation(gjk[1][:, 0:1], gpart[:, 0:1],
                                             AF.Copy), inc=sgam)
            B.w("v", sgam)
            B.w("v", fr[1], fr[1].n)
            for mi, (name, nch, _, _, _) in enumerate(mats):
                c0, n_ = gcols[name]
                B.op("v", lambda e, a=c0, b2=c0 + n_, o=132 + mi:
                     e.reduce_sum(gpart[:, o:o + 1], gpart[:, a:b2], AX.X),
                     inc=svg)
            B.w("t", svg, 3)
            for mi in range(3):
                B.op("t", lambda e, o=132 + mi, b=pbf[mi]:
                     e.matmul(b[:, 0:1], onesf[:, :], gpart[:, o:o + 1],
                              start=True, stop=True), inc=stg)
            B.w("v", stg, 3)
            for mi, (name, nch, _, _, _) in enumerate(mats):
                n_el = OC * 128 * 2048
                B.op("v", lambda e, b=pbf[mi], t_=tau[mi], c=0.5 / n_el:
                     e.tensor_scalar(t_[:, :], b[:, 0:1], c, 0.5 * QEPS,
                                     ALU.mult, ALU.add), inc=svch)
                B.w("v", svch)
                B.op("v", lambda e, t_=tau[mi], u=ntau[mi]:
                     e.tensor_single_scalar(u[:, :], t_[:, :], -1.0, ALU.mult),
                     inc=svg)
            B.w("v", svg, 6)
            B.w("g", svg, 6)

            for mi, (name, nch, srcf, dstf, nd) in enumerate(mats):
                for c in range(nch):
                    p = c % 2
                    eng = "v"
                    chain = svch
                    B.w("sy", fr[p])
                    B.dma("sy", buf_view(wcb[p], nd), srcf(c), ld[p])
                    B.w(eng, ld[p])
                    if qwr[p].n > 0:
                        B.w(eng, qwr[p])
                    B.op(eng, lambda e, b=wcb[p], u=ntau[mi], l=lsb[p]:
                         e.tensor_single_scalar(l[:, :], b[:, :],
                                                u[:, :], ALU.is_lt),
                         inc=chain)
                    B.w(eng, chain)
                    B.op(eng, lambda e, b=wcb[p], t_=tau[mi], q=qb[p], l=lsb[p]:
                         e.scalar_tensor_tensor(q[:, :], b[:, :], t_[:, :],
                                                l[:, :], ALU.is_gt,
                                                ALU.subtract),
                         inc=fr[p])
                    B.w("g", fr[p])
                    B.dma("g", dstf(c), buf_view(qb[p], nd, split_j=(nd == 2)),
                          qwr[p])
            B.barrier([fr[0], fr[1], qwr[0], qwr[1], svg])

        with ExitStack() as p1:
            cosf = _sb(p1, nc, "cosf_sb", [128, T], F32)
            sinf = _sb(p1, nc, "sinf_sb", [128, T], F32)
            wqb_t = _sb(p1, nc, "wqb_t", [128, FC * 2048], BF16)
            ln1f_t = _sb(p1, nc, "ln1f_t", [128, FC * TG], BF16)
            ln1f = [ln1f_t[:, i * TG:(i + 1) * TG] for i in range(FC)]
            xt = [_sb(p1, nc, f"xt{i}", [128, D], F32) for i in (0, 1)]
            ln1t = [_sb(p1, nc, f"ln1t{i}", [128, D], BF16) for i in (0, 1)]
            q_s = [_sb(p1, nc, f"qs{i}", [128, TG], F32) for i in (0, 1, 2)]
            t2 = [_sb(p1, nc, f"t2_{i}", [128, TG], F32) for i in (0, 1, 2)]
            c1 = _sb(p1, nc, "c1", [128, TG], F32)
            t2m = _sb(p1, nc, "t2m", [128, TG], F32)
            qrb = [_sb(p1, nc, f"qrb{i}", [128, TG], BF16) for i in (0, 1, 2)]
            q2b = [_sb(p1, nc, f"q2b{i}", [128, TG], BF16) for i in (0, 1, 2)]
            mtl = _sb(p1, nc, "mtl", [128, TG], F32)
            e2t = _sb(p1, nc, "e2t", [128, TG], F32)
            msqt = _sb(p1, nc, "msqt", [128, TG], F32)
            vart = [_sb(p1, nc, f"vart{i}", [128, TG], F32) for i in (0, 1, 2)]
            stdt = [_sb(p1, nc, f"stdt{i}", [128, TG], F32) for i in (0, 1, 2)]
            rstdt = [_sb(p1, nc, f"rstdt{i}", [128, TG], F32) for i in (0, 1, 2)]
            t1b = _sb(p1, nc, "t1b", [128, TG], BF16)
            t2b = [_sb(p1, nc, f"t2b{i}", [128, TG], BF16) for i in (0, 1, 2)]
            qp = [_sb(p1, nc, f"qp{i}", [128, TG], BF16) for i in (0, 1, 2)]

            dcs = B.sem("p1_dcs")
            B.dma("sy", cosf[:, :], cosf_in, dcs)
            B.dma("sy", sinf[:, :], sinf_in, dcs)
            B.w("v", dcs, 32)

            wql = B.sem("p1_wql")
            xl = [B.sem("p1_xl0"), B.sem("p1_xl1")]
            trw = [B.sem("p1_trw0"), B.sem("p1_trw1")]
            trf = [B.sem("p1_trf0"), B.sem("p1_trf1")]
            stpr = [B.sem(f"p1_stpr{i}") for i in range(3)]
            ssev = [B.sem(f"p1_ssev{i}") for i in range(3)]
            svqr = [B.sem(f"p1_svqr{i}") for i in range(3)]
            ssq2 = [B.sem(f"p1_ssq2{i}") for i in range(3)]
            sts = B.sem("p1_sts0")
            stq2 = B.sem("p1_stq20")
            svs = B.sem("p1_svs0b")
            svq2f = B.sem("p1_svq2f0")
            svvar = [B.sem(f"p1_svvar{i}") for i in range(3)]
            ssstd = [B.sem(f"p1_ssstd{i}") for i in range(3)]
            svt2b = [B.sem(f"p1_svt2b{i}") for i in range(3)]
            ssqp = [B.sem(f"p1_ssqp{i}") for i in range(3)]
            dqwr = [B.sem(f"p1_dqwr{i}") for i in range(3)]
            last_sts = [0, 0, 0]
            last_stq2 = [0, 0, 0]
            dln1w = B.sem("p1_dln1w")
            dln1r = B.sem("p1_dln1r")
            stprog = B.sem("p1_stprog")

            bankfree = [None, None, None]

            def emit_ln1_group(s, g, produce):
                if not produce:
                    B.w("sy", stprog)
                    B.w("sy", dln1w)
                    B.dma("sy",
                          ln1f_t[:, :].rearrange("p (fc w) -> p fc w", fc=FC),
                          s_ln1f[s, g].rearrange("fc p w -> p fc w"), dln1r)
                    return
                first_grp = (s == 0 and g == 0)
                for tt in range(4):
                    p = tt % 2
                    row = s * T + g * TG + tt * 128
                    if xl[p].n > 0:
                        B.w("sy", rln.ssa, rln.ssa.n)
                    B.dma("sy", xt[p][:, :], x_in[row:row + 128, :], xl[p])
                    if not (first_grp and tt < 2):
                        B.w("s", trw[0])
                        B.w("s", trw[1])
                    rln.emit(xt[p][:, :], xt[p][:, :], ln1t[p][:, :], D, epsv,
                             act_prewaits=[(xl[p], xl[p].n)],
                             dve_prewaits=[(xl[p], xl[p].n)])
                    B.w("t", rln.ssa)
                    if tt == 0:
                        B.w("v", stprog)
                        B.w("v", dln1w)
                    for fc in range(FC):
                        q = fc % 2
                        if trw[q].n > 0:
                            B.w("t", trf[q])
                        B.op("t", lambda e, i=ln1t[p], f=fc, b=pbt[q]:
                             e.transpose(b[:, 0:128],
                                         i[:, f * 128:(f + 1) * 128],
                                         ident[:, :]), inc=trw[q])
                        B.w("v", trw[q])
                        B.op("v", lambda e, b=pbt[q], f=fc, o=tt * 128:
                             e.tensor_copy(ln1f_t[:, f * TG + o:f * TG + o + 128],
                                           b[:, 0:128]),
                             inc=trf[q])
                B.w("g", trf[0])
                B.w("g", trf[1])
                B.dma("g", s_ln1f[s, g].rearrange("fc p w -> p fc w"),
                      ln1f_t[:, :].rearrange("p (fc w) -> p fc w", fc=FC),
                      dln1w)

            def emit_pass(kind):
                off = {"q": 0, "k": D, "v": 2 * D}[kind]
                B.w("g", stprog)
                for fq in range(4):
                    B.dma("g",
                          wqb_t[:, fq * 4 * 2048:(fq + 1) * 4 * 2048]
                          .rearrange("p (fc j) -> p fc j", fc=4),
                          wqkv[:, off:off + D]
                          .rearrange("(fc p) j -> p fc j", p=128)
                          [:, fq * 4:(fq + 1) * 4, :], wql)
                hidx = 0
                pend_stats = []

                def flush_stats():
                    while pend_stats:
                        B.prog["t"].append(pend_stats.pop(0))
                for s in range(NSEQ):
                    for g in range(G):
                        emit_ln1_group(s, g, produce=(kind == "q"))
                        B.w("t", wql)
                        if kind == "q":
                            B.w("t", trf[0])
                            B.w("t", trf[1])
                        else:
                            B.w("t", dln1r)
                        sdst = {"q": s_q, "k": s_k, "v": s_v}[kind]
                        for h in range(H):
                            p = hidx % 3
                            hidx += 1
                            if bankfree[p] is not None:
                                B.w("t", *bankfree[p])
                            for fc in range(FC):
                                B.op("t", lambda e, f=fc, hh=h, b=pbf[p],
                                     st=(fc == 0), sp=(fc == FC - 1):
                                     e.matmul(b[:, :],
                                              wqb_t[:, f * 2048 + hh * 128:f * 2048 + (hh + 1) * 128],
                                              ln1f_t[:, f * TG:(f + 1) * TG], start=st, stop=sp),
                                     inc=(stpr[p] if fc == FC - 1 else None))
                            flush_stats()
                            if kind == "v":
                                B.w("s", stpr[p])
                                if dqwr[p].n > 0:
                                    B.w("s", dqwr[p])
                                B.op("s", lambda e, b=pbf[p], o=qp[p]:
                                     e.activation(o[:, :], b[:, :], AF.Copy),
                                     inc=ssqp[p])
                                bankfree[p] = (ssqp[p], ssqp[p].n)
                                B.w("g", ssqp[p])
                                B.dma("g", sdst[s, h, g], qp[p][:, :], dqwr[p])
                                continue
                            B.w("s", stpr[p])
                            if svqr[p].n > 0:
                                B.w("s", svqr[p])
                            B.op("s", lambda e, b=pbf[p], o=q_s[p]:
                                 e.activation(o[:, :], b[:, :], AF.Copy))
                            B.op("s", lambda e, i=q_s[p], o=t2[p]:
                                 e.activation(o[0:64, :], i[64:128, :],
                                              AF.Copy))
                            B.op("s", lambda e, i=q_s[p], o=t2[p]:
                                 e.activation(o[64:128, :], i[0:64, :],
                                              AF.Copy), inc=ssev[p])
                            bankfree[p] = (ssev[p], ssev[p].n)
                            gs = g * TG
                            B.w("v", ssev[p])
                            if ssq2[p].n > 0:
                                B.w("v", ssq2[p])
                                B.w("v", sts, last_sts[p])
                            B.op("v", lambda e, i=q_s[p], a=gs:
                                 e.tensor_mul(c1[:, :], i[:, :],
                                              cosf[:, a:a + TG]))
                            B.op("v", lambda e, i=t2[p], a=gs:
                                 e.tensor_mul(t2m[:, :], i[:, :],
                                              sinf[:, a:a + TG]))
                            B.op("v", lambda e, o=qrb[p]:
                                 e.tensor_add(o[:, :], c1[:, :], t2m[:, :]),
                                 inc=svqr[p])
                            B.w("s", svqr[p])
                            if stq2.n > 0 and last_stq2[p] > 0:
                                B.w("s", stq2, last_stq2[p])
                            B.op("s", lambda e, i=qrb[p], o=q2b[p]:
                                 e.activation(o[:, :], i[:, :], AF.Square),
                                 inc=ssq2[p])
                            w_qr = svqr[p].n
                            w_svs = svs.n
                            w_q2 = ssq2[p].n
                            w_q2f = svq2f.n
                            sts.n += 1
                            stq2.n += 1
                            last_sts[p] = sts.n
                            last_stq2[p] = stq2.n

                            def mk_stats(p=p, w_qr=w_qr, w_svs=w_svs,
                                         w_q2=w_q2, w_q2f=w_q2f):
                                def f1(e):
                                    e.wait_ge(svqr[p].h, w_qr)
                                def f2(e):
                                    if w_svs > 0:
                                        e.wait_ge(svs.h, w_svs)
                                def f3(e, i=qrb[p]):
                                    e.matmul(pbf[3][:, :], ones128[:, :],
                                             i[:, :], start=True, stop=True
                                             ).then_inc(sts.h, 1)
                                def f4(e):
                                    e.wait_ge(ssq2[p].h, w_q2)
                                def f5(e):
                                    if w_q2f > 0:
                                        e.wait_ge(svq2f.h, w_q2f)
                                def f6(e, i=q2b[p]):
                                    e.matmul(pbf[4][:, :], ones128[:, :],
                                             i[:, :], start=True, stop=True
                                             ).then_inc(stq2.h, 1)
                                return [f1, f2, f3, f4, f5, f6]
                            pend_stats.extend(mk_stats())
                            B.w("v", sts, last_sts[p])
                            if ssstd[p].n > 0:
                                B.w("v", ssstd[p])
                            B.op("v", lambda e, b=pbf[3]:
                                 e.tensor_single_scalar(mtl[:, :], b[:, :],
                                                        1.0 / 128, ALU.mult),
                                 inc=svs)
                            B.w("v", stq2, last_stq2[p])
                            B.op("v", lambda e, b=pbf[4]:
                                 e.tensor_single_scalar(e2t[:, :], b[:, :],
                                                        1.0 / 128, ALU.mult),
                                 inc=svq2f)
                            B.w("v", svs)
                            B.op("v", lambda e:
                                 e.scalar_tensor_tensor(msqt[:, :], mtl[:, :],
                                                        1.0, mtl[:, :],
                                                        ALU.mult, ALU.mult))
                            B.op("v", lambda e, o=vart[p]:
                                 e.tensor_sub(o[:, :], e2t[:, :], msqt[:, :]),
                                 inc=svvar[p])
                            B.w("s", svvar[p])
                            if svt2b[p].n > 0:
                                B.w("s", svt2b[p])
                            B.op("s", lambda e, i=vart[p], o=stdt[p]:
                                 e.activation(o[:, :], i[:, :], AF.Ln,
                                              bias=epsv[:, :]))
                            B.op("s", lambda e, i=stdt[p], o=rstdt[p]:
                                 e.activation(o[:, :], i[:, :], AF.Exp,
                                              scale=-0.5), inc=ssstd[p])
                            B.w("v", ssstd[p])
                            if ssqp[p].n > 0:
                                B.w("v", ssqp[p])
                            B.op("v", lambda e, i=qrb[p]:
                                 e.tensor_sub(t1b[:, :], i[:, :], mtl[:, :]))
                            B.op("v", lambda e, o=t2b[p], r=rstdt[p]:
                                 e.tensor_mul(o[:, :], t1b[:, :], r[:, :]),
                                 inc=svt2b[p])
                            B.w("s", svt2b[p])
                            if dqwr[p].n > 0:
                                B.w("s", dqwr[p])
                            B.op("s", lambda e, i=t2b[p], o=qp[p]:
                                 e.activation(o[:, :], i[:, :], AF.Identity,
                                              bias=lnbi[:, :],
                                              scale=lnsc[:, :]), inc=ssqp[p])
                            B.w("g", ssqp[p])
                            B.dma("g", sdst[s, h, g], qp[p][:, :], dqwr[p])
                        flush_stats()
                        B.op("t", lambda e: e.matmul(
                            pbf[5][0:1, 0:1], onescol[:, :], onescol[:, :],
                            start=True, stop=True), inc=stprog)

            emit_pass("q")
            emit_pass("k")
            emit_pass("v")
            B.barrier([stprog, ssqp[0], ssqp[1], ssqp[2],
                       svt2b[0], svt2b[1], svt2b[2],
                       dqwr[0], dqwr[1], dqwr[2],
                       dln1w, dln1r, trf[0], trf[1]])

        with ExitStack() as p34:
            masks = _sb(p34, nc, "masks_sb", [128, 4 * TG], BF16)
            attnf = [_sb(p34, nc, f"af{i}", [128, T], BF16) for i in range(H)]

            dmk = B.sem("p3_dmk")
            B.dma("g", masks[:, :], masks_in, dmk)
            B.w("v", dmk, 16)
            dwo = B.sem("p3_dwo")

            dqkv = [B.sem("p0_ld0"), B.sem("p0_ld1")]
            stvt = [B.sem("p0_fr0"), B.sem("p0_fr1")]
            ssvt = [B.sem("p0_qw0"), B.sem("p0_qw1")]
            stS = [B.sem("p3_stS0"), B.sem("p3_stS1")]
            ssT = [B.sem("p3_ssT0"), B.sem("p3_ssT1")]
            svA = [B.sem("p3_svA0"), B.sem("p3_svA1")]
            stA = [B.sem("p3_stA0"), B.sem("p3_stA1")]
            sto = B.sem("p0_svch")
            stden = B.sem("p0_sgam")
            svden = B.sem("p0_svg")
            stdb = B.sem("p0_stg")
            ssrb = B.sem("p3_ssrb")
            svat = B.sem("p3_svat")
            sthdone = B.sem("p3_sthdone")
            sty = [B.sem("p4_sty0"), B.sem("p4_sty1")]
            ssyev = [B.sem("p4_ssyev0"), B.sem("p4_ssyev1")]
            xl2 = [B.sem("p4_xl20"), B.sem("p4_xl21")]
            svxn = [B.sem("p4_svxn0"), B.sem("p4_svxn1")]
            dxnw = [B.sem("p4_dxnw0"), B.sem("p4_dxnw1")]
            stp4 = B.sem("p4_stp4")

            B.w("t", B.sem("p1_trf0"))
            B.w("t", B.sem("p1_trf1"))

            for s in range(NSEQ):
              with ExitStack() as pA:
                qsb = [_sb(pA, nc, f"qsb{s}_{i}", [128, T], BF16) for i in (0, 1)]
                ksb = [_sb(pA, nc, f"ksb{s}_{i}", [128, T], BF16) for i in (0, 1)]
                vsb = [_sb(pA, nc, f"vsb{s}_{i}", [128, T], BF16) for i in (0, 1)]
                vtt = [_sb(pA, nc, f"vtt{s}_{i}", [128, T], BF16) for i in (0, 1)]
                Ttl = [_sb(pA, nc, f"Ttl{s}_{i}", [128, TG], F32) for i in (0, 1)]
                Abf = [_sb(pA, nc, f"Abf{s}_{i}", [128, TG], BF16) for i in (0, 1)]
                drb = [_sb(pA, nc, f"drb{s}_{i}", [1, TG], BF16) for i in (0, 1)]
                rbt = [_sb(pA, nc, f"rbt{s}_{i}", [128, TG], BF16) for i in (0, 1)]
                if s > 0:
                    B.barrier([stp4, svxn[0], svxn[1], dxnw[0], dxnw[1],
                               rln.ssa, ssyev[0], ssyev[1], sty[0], sty[1]])
                B.w("v", stp4)
                for h in range(H):
                      p = h % 2
                      if dqkv[p].n > 0:
                          B.w("sy", sthdone)
                      B.dma("sy", qsb[p][:, :].rearrange("p (g w) -> p g w", g=G),
                            s_q[s, h].rearrange("g p w -> p g w"), dqkv[p])
                      B.dma("sy", ksb[p][:, :].rearrange("p (g w) -> p g w", g=G),
                            s_k[s, h].rearrange("g p w -> p g w"), dqkv[p])
                      B.dma("sy", vsb[p][:, :].rearrange("p (g w) -> p g w", g=G),
                            s_v[s, h].rearrange("g p w -> p g w"), dqkv[p])
                      B.w("t", dqkv[p])
                      if sthdone.n >= 2:
                          B.w("s", sthdone)
                      for c in range(16):
                          q = c % 2
                          if stvt[q].n > 0:
                              B.w("t", ssvt[q])
                          B.op("t", lambda e, i=vsb[p], c_=c, b=pbt[q]:
                               e.transpose(b[:, 0:128],
                                           i[:, c_ * 128:(c_ + 1) * 128],
                                           ident[:, :]), inc=stvt[q])
                          B.w("s", stvt[q])
                          B.op("s", lambda e, b=pbt[q], o=vtt[p], c_=c:
                               e.activation(o[:, c_ * 128:(c_ + 1) * 128],
                                            b[:, 0:128], AF.Copy), inc=ssvt[q])
                      B.w("t", ssvt[0])
                      B.w("t", ssvt[1])
                      for g in range(G):
                          nch = 4 * g + 4

                          def emit_S(c):
                              cp = c % 2
                              if stS[cp].n > 0:
                                  B.w("t", ssT[cp])
                              B.op("t", lambda e, b=pbf[cp], i=ksb[p], c_=c,
                                   j=qsb[p], a=g * TG:
                                   e.matmul(b[:, :],
                                            i[:, c_ * 128:(c_ + 1) * 128],
                                            j[:, a:a + TG],
                                            start=True, stop=True), inc=stS[cp])
                              B.w("s", stS[cp])
                              if svA[cp].n > 0:
                                  B.w("s", svA[cp])
                              B.op("s", lambda e, b=pbf[cp], o=Ttl[cp]:
                                   e.activation(o[:, :], b[:, :], AF.Square),
                                   inc=ssT[cp])
                              if c >= 4 * g:
                                  j = c - 4 * g
                                  B.w("v", ssT[cp])
                                  if stA[cp].n > 0:
                                      B.w("v", stA[cp])
                                  B.op("v", lambda e, i=Ttl[cp], o=Abf[cp]:
                                       e.tensor_mul(o[:, :], i[:, :], i[:, :]))
                                  B.op("v", lambda e, o=Abf[cp], j_=j:
                                       e.tensor_mul(o[:, :], o[:, :],
                                                    masks[:, j_ * TG:(j_ + 1) * TG]),
                                       inc=svA[cp])
                              else:
                                  if stA[cp].n > 0:
                                      B.w("s", stA[cp])
                                  B.op("s", lambda e, i=Ttl[cp], o=Abf[cp]:
                                       e.activation(o[:, :], i[:, :],
                                                    AF.Square), inc=svA[cp])

                          def emit_AVden(c):
                              cp = c % 2
                              if c == 0:
                                  B.w("t", svat)
                              B.w("t", svA[cp])
                              B.op("t", lambda e, b=pbf[2], i=vtt[p], c_=c,
                                   a=Abf[cp], st=(c == 0), sp=(c == nch - 1):
                                   e.matmul(b[:, :],
                                            i[:, c_ * 128:(c_ + 1) * 128],
                                            a[:, :], start=st, stop=sp),
                                   inc=(sto if c == nch - 1 else None))
                              B.op("t", lambda e, b=pbf[3], a=Abf[cp],
                                   st=(c == 0), sp=(c == nch - 1):
                                   e.matmul(b[0:1, :], onescol[:, :], a[:, :],
                                            start=st, stop=sp),
                                   inc=(stden if c == nch - 1 else stA[cp]))

                          for c in range(nch):
                              emit_S(c)
                              if c >= 1:
                                  emit_AVden(c - 1)
                          emit_AVden(nch - 1)
                          B.w("v", stden)
                          if svden.n >= 2:
                              B.w("v", stdb)
                          B.op("v", lambda e, b=pbf[3], o=drb[g % 2]:
                               e.reciprocal(o[:, :], b[0:1, :]), inc=svden)
                          B.w("t", svden)
                          B.w("t", ssrb)
                          B.op("t", lambda e, b=pbf[4], i=drb[g % 2]:
                               e.matmul(b[:, :], onesrow[:, :], i[:, :],
                                        start=True, stop=True), inc=stdb)
                          B.w("s", stdb)
                          if ssrb.n >= 2:
                              B.w("s", svat)
                          B.op("s", lambda e, b=pbf[4], o=rbt[g % 2]:
                               e.activation(o[:, :], b[:, :], AF.Copy), inc=ssrb)
                          B.w("v", sto)
                          B.w("v", ssrb)
                          B.op("v", lambda e, b=pbf[2], i=rbt[g % 2], o=attnf[h],
                               a=g * TG:
                               e.tensor_mul(o[:, a:a + TG], b[:, :], i[:, :]),
                               inc=svat)
                      B.op("t", lambda e: e.matmul(
                          pbf[5][0:1, 0:1], onescol[:, :], onescol[:, :],
                          start=True, stop=True), inc=sthdone)

              with ExitStack() as pB:
                woutb = [_sb(pB, nc, f"wo{s}_{i}", [128, 2048], BF16)
                         for i in range(FC)]
                yf = [_sb(pB, nc, f"yf{s}_{i}", [128, D], F32) for i in (0, 1)]
                lny = [_sb(pB, nc, f"lny{s}_{i}", [128, D], F32) for i in (0, 1)]
                xt2 = [_sb(pB, nc, f"xt2_{s}_{i}", [128, D], F32) for i in (0, 1)]
                xnw = [_sb(pB, nc, f"xnw{s}_{i}", [128, D], F32) for i in (0, 1)]
                B.barrier([sthdone, svat, ssT[0], ssT[1], ssrb,
                           ssvt[0], ssvt[1]])
                for fc in range(FC):
                    B.dma("g", woutb[fc][:, :],
                          wout[fc * 128:(fc + 1) * 128, :], dwo)
                B.w("t", dwo)
                B.w("t", svat)
                for ti in range(16):
                    trow = s * T + ti * 128
                    pp = ti % 2
                    for jg in range(4):
                        pj = jg % 2
                        if sty[pj].n > 0:
                            B.w("t", ssyev[pj])
                        for h in range(H):
                            B.op("t", lambda e, b=pbf[pj], i=attnf[h], t_=ti,
                                 hh=h, jj=jg, st=(h == 0), sp=(h == H - 1):
                                 e.matmul(b[:, :],
                                          i[:, t_ * 128:(t_ + 1) * 128],
                                          woutb[hh][:, jj * TG:(jj + 1) * TG],
                                          start=st, stop=sp),
                                 inc=(sty[pj] if h == H - 1 else None))
                        B.w("s", sty[pj])
                        if ti >= 2 and jg == 0:
                            B.w("s", rln.ssa)
                        B.op("s", lambda e, b=pbf[pj], o=yf[pp], jj=jg:
                             e.activation(o[:, jj * TG:(jj + 1) * TG], b[:, :],
                                          AF.Copy), inc=ssyev[pj])
                    if xl2[pp].n > 0:
                        B.w("sy", svxn[pp])
                    B.dma("sy", xt2[pp][:, :], x_in[trow:trow + 128, :],
                          xl2[pp])
                    rln.emit(yf[pp][:, :], yf[pp][:, :], lny[pp][:, :], D, epsv,
                             act_prewaits=[(ssyev[0], ssyev[0].n),
                                           (ssyev[1], ssyev[1].n),
                                           (svxn[pp], svxn[pp].n)],
                             dve_prewaits=[(ssyev[0], ssyev[0].n),
                                           (ssyev[1], ssyev[1].n)])
                    B.w("v", rln.ssa)
                    B.w("v", xl2[pp])
                    if dxnw[pp].n > 0:
                        B.w("v", dxnw[pp])
                    B.op("v", lambda e, o=xnw[pp], a=lny[pp], b2=xt2[pp]:
                         e.tensor_add(o[:, :], a[:, :], b2[:, :]),
                         inc=svxn[pp])
                    B.w("g", svxn[pp])
                    B.dma("g", s_xnew[trow:trow + 128, :], xnw[pp][:, :],
                          dxnw[pp])
                B.op("t", lambda e: e.matmul(
                    pbf[5][0:1, 0:1], onescol[:, :], onescol[:, :],
                    start=True, stop=True), inc=stp4)
            B.barrier([stp4, svat, svxn[0], svxn[1], dxnw[0], dxnw[1],
                       sthdone, ssvt[0], ssvt[1], rln.ssa])

        with ExitStack() as p67:
            ln2f = [_sb(p67, nc, f"ln2f{i}", [128, NSEQ * T], BF16)
                    for i in range(FC)]
            xt3 = [_sb(p67, nc, f"xt3_{i}", [128, D], F32) for i in (0, 1)]
            ln2t = [_sb(p67, nc, f"ln2t{i}", [128, D], BF16) for i in (0, 1)]

            xl3 = [B.sem("p1_xl0"), B.sem("p1_xl1")]
            trw2 = [B.sem("p1_trw0"), B.sem("p1_trw1")]
            trf2 = [B.sem("p1_trf0"), B.sem("p1_trf1")]

            for i in range(NSEQ * T // 128):
                p = i % 2
                if xl3[p].n > 0:
                    B.w("sy", rln.ssa, rln.ssa.n)
                B.dma("sy", xt3[p][:, :], s_xnew[i * 128:(i + 1) * 128, :],
                      xl3[p])
                if i >= 2:
                    B.w("s", trw2[0])
                    B.w("s", trw2[1])
                rln.emit(xt3[p][:, :], xt3[p][:, :], ln2t[p][:, :], D, epsv,
                         act_prewaits=[(xl3[p], xl3[p].n)],
                         dve_prewaits=[(xl3[p], xl3[p].n)])
                B.w("t", rln.ssa)
                for fc in range(FC):
                    q = fc % 2
                    if trw2[q].n > 0:
                        B.w("t", trf2[q])
                    B.op("t", lambda e, i_=ln2t[p], f=fc, b=pbt[q]:
                         e.transpose(b[:, 0:128], i_[:, f * 128:(f + 1) * 128],
                                     ident[:, :]), inc=trw2[q])
                    B.w("v", trw2[q])
                    B.op("v", lambda e, b=pbt[q], f=fc, o=i * 128:
                         e.tensor_copy(ln2f[f][:, o:o + 128], b[:, 0:128]),
                         inc=trf2[q])

            w1c = [_sb(p67, nc, f"w1c{i}", [128, 2048], BF16) for i in (0, 1)]
            w2c = [_sb(p67, nc, f"w2c{i}", [128, 2048], BF16) for i in (0, 1)]
            sgc1 = _sb(p67, nc, "sgc", [128, NSEQ * T], BF16)
            sgc = [sgc1, sgc1]
            hdc1 = _sb(p67, nc, "hdc", [128, NSEQ * T], BF16)
            hdc = [hdc1, hdc1]

            wc1l = [B.sem("p0_ld0"), B.sem("p0_ld1")]
            wc2l = [B.sem("dC"), B.sem("p1_dcs")]
            stx1 = B.sem("p1_sts0")
            stx2 = B.sem("p1_stq20")
            sssg = B.sem("p1_ssq20")
            svhd = B.sem("p1_svqr0")
            dhw = [B.sem("p1_dqwr0"), B.sem("p1_dqwr1")]
            stw1f = [B.sem("p1_stpr0"), B.sem("p1_stpr1")]

            B.w("t", trf2[0])
            B.w("t", trf2[1])
            NB = NSEQ * T // TG
            for oc in range(OC):
                p = oc % 2
                if wc1l[p].n > 0:
                    B.w("sy", stw1f[p])
                B.dma("sy", w1c[p][:, :].rearrange("p (fc o) -> p fc o", fc=FC),
                      s_w1q[oc].rearrange("fc p o -> p fc o"), wc1l[p])
                B.dma("sy", w2c[p][:, :].rearrange("p (fc o) -> p fc o", fc=FC),
                      s_w2q[oc].rearrange("fc p o -> p fc o"), wc2l[p])
                B.w("t", wc1l[p])
                B.w("t", wc2l[p])
                for b in range(NB):
                    bb = b % 2
                    if stx1.n >= 2:
                        B.w("t", sssg)
                    for fc in range(FC):
                        B.op("t", lambda e, f=fc, bk=pbf[bb], w=w1c[p], b_=b,
                             st=(fc == 0), sp=(fc == FC - 1):
                             e.matmul(bk[:, :], w[:, f * 128:(f + 1) * 128],
                                      ln2f[f][:, b_ * TG:(b_ + 1) * TG],
                                      start=st, stop=sp),
                             inc=(stx1 if fc == FC - 1 else None))
                    if stx2.n >= 2:
                        B.w("t", svhd)
                    for fc in range(FC):
                        B.op("t", lambda e, f=fc, bk=pbf[2 + bb], w=w2c[p],
                             b_=b, st=(fc == 0), sp=(fc == FC - 1):
                             e.matmul(bk[:, :], w[:, f * 128:(f + 1) * 128],
                                      ln2f[f][:, b_ * TG:(b_ + 1) * TG],
                                      start=st, stop=sp),
                             inc=(stx2 if fc == FC - 1 else None))
                    B.w("s", stx1)
                    if oc >= 1 and b == 0:
                        B.w("s", svhd)
                    B.op("s", lambda e, bk=pbf[bb], o=sgc[p], b_=b:
                         e.activation(o[:, b_ * TG:(b_ + 1) * TG], bk[:, :],
                                      AF.Sigmoid), inc=sssg)
                    B.w("v", sssg)
                    B.w("v", stx2)
                    if b == 0:
                        B.w("v", dhw[0])
                        B.w("v", dhw[1])
                    B.op("v", lambda e, bk=pbf[2 + bb], i=sgc[p], o=hdc[p],
                         b_=b:
                         e.tensor_mul(o[:, b_ * TG:(b_ + 1) * TG],
                                      i[:, b_ * TG:(b_ + 1) * TG], bk[:, :]),
                         inc=svhd)
                B.w("g", svhd)
                B.dma("g", s_hid[:, oc].rearrange("tb p t -> p tb t"),
                      hdc[p][:, :].rearrange("p (tb t) -> p tb t", tb=8),
                      dhw[p])
                B.op("t", lambda e: e.matmul(
                    pbf[5][0:1, 0:1], onescol[:, :], onescol[:, :],
                    start=True, stop=True), inc=stw1f[p])
            B.barrier([stw1f[0], stw1f[1], dhw[0], dhw[1], svhd, sssg,
                       trf2[0], trf2[1], rln.ssa])

        with ExitStack() as p8:
            TB = 512
            NTB = NSEQ * T // TB
            hidsl = _sb(p8, nc, "hidsl", [128, OC * TB], BF16)
            w3sl = [_sb(p8, nc, f"w3sl{i}", [128, OC * TG], BF16) for i in (0, 1)]
            y3b = [_sb(p8, nc, f"y3b{i}", [128, D], BF16) for i in range(TB // 128)]
            lny3 = [_sb(p8, nc, f"lny3{i}", [128, D], F32) for i in (0, 1)]
            xt4 = [_sb(p8, nc, f"xt4_{i}", [128, D], F32) for i in (0, 1)]
            outt = [_sb(p8, nc, f"outt{i}", [128, D], F32) for i in (0, 1)]

            hbl = B.sem("p1_dln1r")
            w3l = [B.sem("p1_xl0"), B.sem("p1_xl1")]
            stx3 = B.sem("p3_stS0")
            ssy3 = B.sem("p3_ssT0")
            xl4 = [B.sem("p4_xl20"), B.sem("p4_xl21")]
            svout = [B.sem("p4_svxn0"), B.sem("p4_svxn1")]
            dy = B.sem("p8_dy")
            stblk = B.sem("p3_sthdone")

            jg_marks = {}
            w3_loaded = {}

            def emit_w3_load(gj):
                if gj in w3_loaded or gj >= 4 * NTB:
                    return
                pj = gj % 2
                if gj >= 2:
                    B.w("sy", stblk, jg_marks[gj - 2])
                jg_ = gj % 4
                B.dma("sy",
                      w3sl[pj][:, :].rearrange("p (hc j) -> p hc j", hc=OC),
                      s_w3q[:, jg_].rearrange("hc p j -> p hc j"), w3l[pj])
                w3_loaded[gj] = w3l[pj].n

            for tb in range(NTB):
                if tb > 0:
                    B.w("sy", stblk)
                B.dma("sy",
                      hidsl[:, :].rearrange("p (hc t) -> p hc t", hc=OC),
                      s_hid[tb].rearrange("hc p t -> p hc t"), hbl)
                for jg in range(4):
                    gj = tb * 4 + jg
                    emit_w3_load(gj)
                    emit_w3_load(gj + 1)
                    pj = gj % 2
                    B.w("t", hbl)
                    B.w("t", w3l[pj], w3_loaded[gj])
                    for tt in range(TB // 128):
                        bb = tt % 2
                        if stx3.n >= 2:
                            B.w("t", ssy3)
                        for hc in range(OC):
                            B.op("t", lambda e, b_=pbf[bb], h_=hc, t_=tt,
                                 pj_=pj, st=(hc == 0), sp=(hc == OC - 1):
                                 e.matmul(b_[:, :],
                                          hidsl[:, h_ * TB + t_ * 128:
                                                h_ * TB + (t_ + 1) * 128],
                                          w3sl[pj_][:, h_ * TG:(h_ + 1) * TG],
                                          start=st, stop=sp),
                                 inc=(stx3 if hc == OC - 1 else None))
                        B.w("s", stx3)
                        B.op("s", lambda e, b_=pbf[bb], o=y3b[tt], jj=jg:
                             e.activation(o[:, jj * TG:(jj + 1) * TG], b_[:, :],
                                          AF.Copy), inc=ssy3)
                    B.op("t", lambda e: e.matmul(
                        pbf[5][0:1, 0:1], onescol[:, :], onescol[:, :],
                        start=True, stop=True), inc=stblk)
                    jg_marks[gj] = stblk.n
                for tt in range(TB // 128):
                    pp = tt % 2
                    row = tb * TB + tt * 128
                    if xl4[pp].n > 0:
                        B.w("sy", svout[pp])
                    B.dma("sy", xt4[pp][:, :], s_xnew[row:row + 128, :],
                          xl4[pp])
                    rln.emit(y3b[tt][:, :], y3b[tt][:, :], lny3[pp][:, :], D,
                             epsv,
                             act_prewaits=[(ssy3, ssy3.n),
                                           (svout[pp], svout[pp].n)],
                             dve_prewaits=[(ssy3, ssy3.n)])
                    B.w("v", rln.ssa)
                    B.w("v", xl4[pp])
                    if dy.n >= 32:
                        B.w("v", dy, dy.n - 16)
                    B.op("v", lambda e, o=outt[pp], a=lny3[pp], b2=xt4[pp]:
                         e.tensor_add(o[:, :], a[:, :], b2[:, :]),
                         inc=svout[pp])
                    B.w("sy", svout[pp])
                    B.dma("sy", y_out[row:row + 128, :], outt[pp][:, :], dy)
            B.w("sy", dy)

        with nc.allow_low_precision(
                "bf16 matmul operands / bf16 rounding blessed for this problem"
        ), nc.Block() as block:
            @block.sync
            def _(e):
                for f in B.prog["sy"]:
                    f(e)

            @block.gpsimd
            def _(e):
                for f in B.prog["g"]:
                    f(e)

            @block.tensor
            def _(e):
                for f in B.prog["t"]:
                    f(e)

            @block.scalar
            def _(e):
                for f in B.prog["s"]:
                    f(e)

            @block.vector
            def _(e):
                for f in B.prog["v"]:
                    f(e)

    return nc


def kernel(L0_attn_x, L0_attn_w_qkv, L0_attn_w_out, L0_attn_ln_scale,
           L0_attn_ln_bias, L0_ffn_w1, L0_ffn_w2, L0_ffn_w3):
    x = np.ascontiguousarray(np.asarray(L0_attn_x, dtype=np.float32))
    wqkv_a = np.ascontiguousarray(np.asarray(L0_attn_w_qkv, dtype=np.float32))
    wout_a = np.ascontiguousarray(np.asarray(L0_attn_w_out, dtype=np.float32))
    lnsc_a = np.ascontiguousarray(
        np.asarray(L0_attn_ln_scale, dtype=np.float32).reshape(128, 1))
    lnbi_a = np.ascontiguousarray(
        np.asarray(L0_attn_ln_bias, dtype=np.float32).reshape(128, 1))
    w1 = np.asarray(L0_ffn_w1, dtype=np.float32)
    w2 = np.asarray(L0_ffn_w2, dtype=np.float32)
    w3 = np.asarray(L0_ffn_w3, dtype=np.float32)

    w1tt_a = np.ascontiguousarray(
        w1.reshape(OC, 128, FC, 128).transpose(0, 2, 3, 1))
    w2tt_a = np.ascontiguousarray(
        w2.reshape(OC, 128, FC, 128).transpose(0, 2, 3, 1))
    w3t_a = np.ascontiguousarray(w3.T)

    pos = np.arange(T, dtype=np.float64)
    inv = 10000.0 ** (-np.arange(0, HD, 2, dtype=np.float64) / HD)
    ang = pos[:, None] * inv[None, :]
    c = np.cos(ang).astype(np.float32).T
    sn = np.sin(ang).astype(np.float32).T
    cosf_a = np.ascontiguousarray(np.concatenate([c, c], axis=0))
    sinf_a = np.ascontiguousarray(np.concatenate([-sn, sn], axis=0))
    ident_a = np.eye(128, dtype=np.float32)
    tk = np.arange(128)[:, None]
    tq = np.arange(TG)[None, :]
    masks_a = np.ascontiguousarray(
        np.stack([(tk <= tq - 128 * j).astype(np.float32) for j in range(4)],
                 axis=1).reshape(128, 4 * TG))

    nc = build_program()
    in_maps = []
    for core in range(NCORES):
        xs = np.ascontiguousarray(
            x[2 * core:2 * core + 2].reshape(NSEQ * T, D))
        in_maps.append({
            "x": xs, "wqkv": wqkv_a, "wout": wout_a,
            "lnsc": lnsc_a, "lnbi": lnbi_a,
            "w1tt": w1tt_a, "w2tt": w2tt_a, "w3t": w3t_a,
            "cosf": cosf_a, "sinf": sinf_a, "identm": ident_a,
            "masks": masks_a,
        })
    res = run_bass_kernel_spmd(nc, in_maps, core_ids=list(range(NCORES)))
    y = np.concatenate(
        [r["y"].reshape(NSEQ, T, D) for r in res.results], axis=0)
    return y
